# version 01
# baseline (speedup 1.0000x reference)
import contextlib
import numpy as np

import concourse.bass as bass
import concourse.mybir as mybir
from concourse.bass_utils import run_bass_kernel_spmd

try:
    import ml_dtypes
    BFDT = ml_dtypes.bfloat16
except Exception:
    import jax.numpy as jnp
    BFDT = jnp.bfloat16

F32 = mybir.dt.float32
BF16 = mybir.dt.bfloat16
AF = mybir.ActivationFunctionType
ALU = mybir.AluOpType
AX = mybir.AxisListType.X

ENGINES = ("sync", "tensor", "vector", "scalar")

D = 4096
TOK = 1024
NTT = 8
NKD = 32


class Seq:

    def __init__(self):
        self.items = {e: [] for e in ENGINES}
        self.cum = {e: 0 for e in ENGINES}
        self.last = None

    def op(self, eng, fn, inc=1, wait=True):
        if wait and self.last is not None:
            self.items[eng].append(("w", self.last[0], self.last[1]))
        self.cum[eng] += inc
        self.items[eng].append(("i", fn, inc))
        self.last = (eng, self.cum[eng])

    def dma(self, out, in_, wait=True):
        self.op("sync", lambda e, o=out, i=in_: e.dma_start(o, i), inc=16, wait=wait)

    def fin(self):
        self.items["scalar"].append(("w", self.last[0], self.last[1]))


def _emit(nc, S, es):
    sems = {e: es.enter_context(nc.semaphore(name="sem_" + e)) for e in ENGINES}
    with nc.Block() as block:
        for ename in ENGINES:
            items = S.items[ename]
            if not items:
                continue

            def make(items=items, ename=ename):
                def _(eng):
                    for it in items:
                        if it[0] == "w":
                            eng.wait_ge(sems[it[1]], it[2])
                        else:
                            ins = it[1](eng)
                            ins.then_inc(sems[ename], it[2])
                return _
            getattr(block, ename)(make())


class B:
    pass


def _alloc_common(nc, es, bufs, with_lnT=True):
    sb = lambda name, shape, dt: es.enter_context(nc.sbuf_tensor(name, shape, dt))
    bufs.x = sb("x_sb", [128, 4096], F32)
    bufs.xc = sb("xc_sb", [128, 4096], F32)
    bufs.lnb = sb("lnb_sb", [128, 4096], BF16)
    if with_lnT:
        bufs.lnT = sb("lnT_sb", [128, NKD * 1024], BF16)
    bufs.mu = sb("mu_sb", [128, 4], F32)
    bufs.stage = sb("stage_sb", [128, 2048], F32)
    bufs.wtb = sb("wtb_sb", [128, NKD * 512], BF16)
    bufs.idb = sb("idb_sb", [128, 128], BF16)
    return sb


def emit_consts(nc, es, S, vals):
    cp = es.enter_context(nc.sbuf_tensor("cpool", [128, max(1, len(vals))], F32))
    for i, v in enumerate(vals):
        S.op('vector', lambda e, i=i, v=v: e.memset(cp[:, i:i + 1], v), wait=(i == 0))
        nc.const_aps.register(F32, v, cp[:, i:i + 1])


LN_CONSTS = (1e-5, 1.0 / 4096, 1.0 / 32, 1e-10, 1.0 / float(np.sqrt(128.0)))


def emit_ln_phase(S, xn_ap, b, per_t=None):
    for t in range(NTT):
        S.dma(b.x[:, :], xn_ap[t * 128:(t + 1) * 128, :])
        S.op('vector', lambda e, b=b: e.reduce_sum(b.mu[:, 0:1], b.x[:, :], AX))
        S.op('scalar', lambda e, b=b: e.mul(b.mu[:, 1:2], b.mu[:, 0:1], 1.0 / 4096))
        S.op('vector', lambda e, b=b: e.tensor_single_scalar(b.xc[:, :], b.x[:, :], b.mu[:, 1:2], ALU.subtract))
        S.op('scalar', lambda e, b=b: e.activation(b.x[:, :], b.xc[:, :], AF.Square))
        S.op('vector', lambda e, b=b: e.reduce_sum(b.mu[:, 2:3], b.x[:, :], AX))
        S.op('scalar', lambda e, b=b: e.activation(b.mu[:, 0:1], b.mu[:, 2:3], AF.Sqrt, bias=1e-5, scale=1.0 / 4096))
        S.op('vector', lambda e, b=b: e.reciprocal(b.mu[:, 3:4], b.mu[:, 0:1]))
        S.op('vector', lambda e, b=b: e.tensor_single_scalar(b.lnb[:, :], b.xc[:, :], b.mu[:, 3:4], ALU.mult))
        for dt in range(NKD):
            S.op('tensor', lambda e, b=b, dt=dt: e.transpose(b.PT[:, :], b.lnb[:, dt * 128:(dt + 1) * 128], b.idb[:, :]))
            S.op('scalar', lambda e, b=b, dt=dt, t=t: e.copy(
                b.lnT[:, dt * 1024 + t * 128: dt * 1024 + t * 128 + 128], b.PT[:, :]))
        if per_t is not None:
            per_t(t)


def emit_gemm(S, wt_ap, n_oc, b, lhsT_fn, pre_oc=None, post=None, dequant=False, n_tt=NTT, n_kd=NKD):
    for oc in range(n_oc):
        for q in range(n_kd // 4):
            for j in range(4):
                dt = q * 4 + j
                S.dma(b.stage[:, j * 512:(j + 1) * 512],
                      wt_ap[dt * 128:(dt + 1) * 128, oc * 512:(oc + 1) * 512], wait=(j == 0))
            if not dequant:
                S.op('vector', lambda e, b=b, q=q: e.tensor_copy(b.wtb[:, q * 2048:(q + 1) * 2048], b.stage[:, :]))
            else:
                first = True
                for j in range(4):
                    dt = q * 4 + j
                    for ob in range(4):
                        og = oc * 4 + ob
                        S.op('vector', lambda e, b=b, dt=dt, j=j, ob=ob, og=og: e.tensor_single_scalar(
                            b.wtb[:, dt * 512 + ob * 128: dt * 512 + ob * 128 + 128],
                            b.stage[:, j * 512 + ob * 128: j * 512 + ob * 128 + 128],
                            b.sc[:, dt * 32 + og: dt * 32 + og + 1], ALU.mult), wait=first)
                        first = False
        if pre_oc is not None:
            pre_oc(oc)
        for tt in range(n_tt):
            for dt in range(n_kd):
                ap = lhsT_fn(tt, dt)
                S.op('tensor', lambda e, b=b, ap=ap, dt=dt: e.matmul(
                    b.PG[:, :], ap, b.wtb[:, dt * 512:(dt + 1) * 512],
                    start=(dt == 0), stop=(dt == n_kd - 1)), wait=(dt == 0))
            post(tt, oc)


def load_conv(S, b, hbm_ap, dst, total_cols, tile_cols):
    per = 2048 // tile_cols
    ntiles = total_cols // tile_cols
    done = 0
    while done < ntiles:
        n = min(per, ntiles - done)
        for j in range(n):
            dt = done + j
            S.dma(b.stage[:, j * tile_cols:(j + 1) * tile_cols],
                  hbm_ap[dt * 128:(dt + 1) * 128, :], wait=(j == 0))
        S.op('vector', lambda e, b=b, done=done, n=n, tc=tile_cols: e.tensor_copy(
            dst[:, done * tc: (done + n) * tc], b.stage[:, :n * tc]))
        done += n


def build_p1():
    nc = bass.Bass("TRN2", target_bir_lowering=False)
    xn = nc.dram_tensor("xn", [TOK, D], F32, kind="ExternalInput").ap()
    wT = nc.dram_tensor("wT", [D, D], F32, kind="ExternalInput").ap()
    gw1T = nc.dram_tensor("gw1T", [D, 32], F32, kind="ExternalInput").ap()
    gw2T = nc.dram_tensor("gw2T", [32, 8], F32, kind="ExternalInput").ap()
    gb1b = nc.dram_tensor("gb1b", [128, 32], F32, kind="ExternalInput").ap()
    gamb = nc.dram_tensor("gamb", [128, 32], F32, kind="ExternalInput").ap()
    betb = nc.dram_tensor("betb", [128, 32], F32, kind="ExternalInput").ap()
    gb2b = nc.dram_tensor("gb2b", [128, 8], F32, kind="ExternalInput").ap()
    lAf = nc.dram_tensor("lAf", [D, 128], F32, kind="ExternalInput").ap()
    idb_h = nc.dram_tensor("idb", [128, 128], BF16, kind="ExternalInput").ap()
    onesf_h = nc.dram_tensor("onesf", [128, 1], F32, kind="ExternalInput").ap()
    main = nc.dram_tensor("main", [TOK, D], F32, kind="ExternalOutput").ap()
    tmp = nc.dram_tensor("tmp", [TOK, 128], F32, kind="ExternalOutput").ap()
    wg = nc.dram_tensor("wg", [TOK, 16], F32, kind="ExternalOutput").ap()
    pc = nc.dram_tensor("pc", [1, 16], F32, kind="ExternalOutput").ap()

    es = contextlib.ExitStack()
    with es:
        b = B()
        sb = _alloc_common(nc, es, b)
        ps = lambda name, shape, dt=F32: es.enter_context(nc.psum_tensor(name, shape, dt))
        b.PG = ps("PG", [128, 512]); b.PT = ps("PT", [128, 128], BF16); b.PH = ps("PH", [128, 32])
        b.PT32 = ps("PT32", [128, 128], BF16); b.P8 = ps("P8", [128, 8]); b.PC = ps("PC", [128, 16])
        b.PTMP = ps("PTMP", [128, 128])
        gw1b = sb("gw1b", [128, NKD * 32], BF16)
        gw2b = sb("gw2b", [128, 8], BF16)
        lab = sb("lab", [128, NKD * 128], BF16)
        gb1s = sb("gb1s", [128, 32], F32); gams = sb("gams", [128, 32], F32)
        bets = sb("bets", [128, 32], F32); gb2s = sb("gb2s", [128, 8], F32)
        onesf = sb("onesfs", [128, 1], F32)
        oh = sb("oh", [128, 128], F32)
        h32 = sb("h32", [128, 32], F32); h32c = sb("h32c", [128, 32], F32)
        h32b = sb("h32b", [128, 32], BF16); h32T = sb("h32T", [128, 128], BF16)
        g8 = sb("g8", [128, 8], F32); m4 = sb("m4", [128, 4], F32)
        wgs = sb("wgs", [128, 16], F32); tev = sb("tev", [128, 128], F32)
        ev = sb("ev", [128, 512], F32)

        S = Seq()
        emit_consts(nc, es, S, LN_CONSTS)
        S.dma(b.idb[:, :], idb_h[:, :])
        S.dma(onesf[:, :], onesf_h[:, :], wait=False)
        S.dma(gb1s[:, :], gb1b[:, :], wait=False)
        S.dma(gams[:, :], gamb[:, :], wait=False)
        S.dma(bets[:, :], betb[:, :], wait=False)
        S.dma(gb2s[:, :], gb2b[:, :], wait=False)
        for dt in range(NKD):
            S.dma(b.stage[:, dt * 32:(dt + 1) * 32], gw1T[dt * 128:(dt + 1) * 128, :], wait=(dt == 0))
        S.op('vector', lambda e: e.tensor_copy(gw1b[:, :], b.stage[:, :NKD * 32]))
        S.dma(b.stage[0:32, 0:8], gw2T[0:32, 0:8])
        S.op('vector', lambda e: e.tensor_copy(gw2b[0:32, :], b.stage[0:32, 0:8]))
        load_conv(S, b, lAf, lab, NKD * 128, 128)

        def gating(t):
            for dt in range(NKD):
                S.op('tensor', lambda e, dt=dt, t=t: e.matmul(
                    b.PH[:, :], b.lnT[:, dt * 1024 + t * 128: dt * 1024 + t * 128 + 128],
                    gw1b[:, dt * 32:(dt + 1) * 32], start=(dt == 0), stop=(dt == NKD - 1)), wait=(dt == 0))
            S.op('vector', lambda e: e.tensor_add(h32[:, :], b.PH[:, :], gb1s[:, :]))
            S.op('vector', lambda e: e.reduce_sum(m4[:, 0:1], h32[:, :], AX))
            S.op('scalar', lambda e: e.mul(m4[:, 1:2], m4[:, 0:1], 1.0 / 32))
            S.op('vector', lambda e: e.tensor_single_scalar(h32c[:, :], h32[:, :], m4[:, 1:2], ALU.subtract))
            S.op('scalar', lambda e: e.activation(h32[:, :], h32c[:, :], AF.Square))
            S.op('vector', lambda e: e.reduce_sum(m4[:, 2:3], h32[:, :], AX))
            S.op('scalar', lambda e: e.activation(m4[:, 0:1], m4[:, 2:3], AF.Sqrt, bias=1e-5, scale=1.0 / 32))
            S.op('vector', lambda e: e.reciprocal(m4[:, 3:4], m4[:, 0:1]))
            S.op('vector', lambda e: e.tensor_single_scalar(h32[:, :], h32c[:, :], m4[:, 3:4], ALU.mult))
            S.op('vector', lambda e: e.tensor_mul(h32c[:, :], h32[:, :], gams[:, :]))
            S.op('vector', lambda e: e.tensor_add(h32[:, :], h32c[:, :], bets[:, :]))
            S.op('scalar', lambda e: e.activation(h32c[:, :], h32[:, :], AF.Relu))
            S.op('vector', lambda e: e.tensor_copy(h32b[:, :], h32c[:, :]))
            S.op('tensor', lambda e: e.transpose(b.PT32[0:32, 0:128], h32b[:, :], b.idb[:, :]))
            S.op('scalar', lambda e: e.copy(h32T[0:32, :], b.PT32[0:32, 0:128]))
            S.op('tensor', lambda e: e.matmul(b.P8[:, :], h32T[0:32, 0:128], gw2b[0:32, :], start=True, stop=True))
            S.op('vector', lambda e: e.tensor_add(g8[:, :], b.P8[:, :], gb2s[:, :]))
            S.op('vector', lambda e: e.reduce_max(m4[:, 0:1], g8[:, :], AX))
            S.op('vector', lambda e, t=t: e.tensor_single_scalar(oh[:, t * 16:t * 16 + 8], g8[:, :], m4[:, 0:1], ALU.is_equal))
            S.op('vector', lambda e, t=t: e.scalar_tensor_tensor(
                wgs[:, 0:8], oh[:, t * 16:t * 16 + 8], -1e9, g8[:, :], ALU.mult, ALU.add))
            S.op('vector', lambda e: e.reduce_max(m4[:, 1:2], wgs[:, 0:8], AX))
            S.op('vector', lambda e, t=t: e.tensor_single_scalar(oh[:, t * 16 + 8:t * 16 + 16], wgs[:, 0:8], m4[:, 1:2], ALU.is_equal))
            S.op('vector', lambda e: e.tensor_sub(m4[:, 2:3], m4[:, 0:1], m4[:, 1:2]))
            S.op('scalar', lambda e: e.activation(m4[:, 3:4], m4[:, 2:3], AF.Sigmoid))
            S.op('vector', lambda e, t=t: e.tensor_single_scalar(wgs[:, 0:8], oh[:, t * 16:t * 16 + 8], m4[:, 3:4], ALU.mult))
            S.op('vector', lambda e: e.tensor_scalar(m4[:, 2:3], m4[:, 3:4], -1.0, 1.0, ALU.mult, ALU.add))
            S.op('vector', lambda e, t=t: e.tensor_single_scalar(wgs[:, 8:16], oh[:, t * 16 + 8:t * 16 + 16], m4[:, 2:3], ALU.mult))
            S.dma(wg[t * 128:(t + 1) * 128, :], wgs[:, :])
            for dt in range(NKD):
                S.op('tensor', lambda e, dt=dt, t=t: e.matmul(
                    b.PTMP[:, :], b.lnT[:, dt * 1024 + t * 128: dt * 1024 + t * 128 + 128],
                    lab[:, dt * 128:(dt + 1) * 128], start=(dt == 0), stop=(dt == NKD - 1)), wait=(dt == 0))
            S.op('scalar', lambda e: e.copy(tev[:, :], b.PTMP[:, :]))
            S.dma(tmp[t * 128:(t + 1) * 128, :], tev[:, :])

        emit_ln_phase(S, xn, b, per_t=gating)

        for t in range(NTT):
            S.op('tensor', lambda e, t=t: e.matmul(b.PC[0:1, 0:8], onesf[:, 0:1], oh[:, t * 16:t * 16 + 8],
                                                   start=(t == 0), stop=(t == NTT - 1)), wait=(t == 0))
        for t in range(NTT):
            S.op('tensor', lambda e, t=t: e.matmul(b.PC[0:1, 8:16], onesf[:, 0:1], oh[:, t * 16 + 8:t * 16 + 16],
                                                   start=(t == 0), stop=(t == NTT - 1)), wait=(t == 0))
        S.op('vector', lambda e: e.tensor_copy(wgs[0:1, 0:16], b.PC[0:1, 0:16]))
        S.dma(pc[0:1, :], wgs[0:1, 0:16])

        def post(tt, oc):
            S.op('scalar', lambda e: e.copy(ev[:, :], b.PG[:, :]))
            S.dma(main[tt * 128:(tt + 1) * 128, oc * 512:(oc + 1) * 512], ev[:, :])

        emit_gemm(S, wT, 8, b,
                  lambda tt, dt: b.lnT[:, dt * 1024 + tt * 128: dt * 1024 + tt * 128 + 128],
                  post=post)
        S.fin()
        _emit(nc, S, es)
    return nc


def build_p2():
    nc = bass.Bass("TRN2", target_bir_lowering=False)
    xn = nc.dram_tensor("xn", [TOK, D], F32, kind="ExternalInput").ap()
    main = nc.dram_tensor("main", [TOK, D], F32, kind="ExternalInput").ap()
    tmp = nc.dram_tensor("tmp", [TOK, 128], F32, kind="ExternalInput").ap()
    wg = nc.dram_tensor("wg", [TOK, 16], F32, kind="ExternalInput").ap()
    pcall = nc.dram_tensor("pcall", [8, 16], F32, kind="ExternalInput").ap()
    lBf = nc.dram_tensor("lBf", [128, D], F32, kind="ExternalInput").ap()
    idb_h = nc.dram_tensor("idb", [128, 128], BF16, kind="ExternalInput").ap()
    onesf_h = nc.dram_tensor("onesf", [128, 1], F32, kind="ExternalInput").ap()
    onesr_h = nc.dram_tensor("onesr", [1, 128], F32, kind="ExternalInput").ap()
    out = nc.dram_tensor("out", [TOK, D], F32, kind="ExternalOutput").ap()

    es = contextlib.ExitStack()
    with es:
        sb = lambda name, shape, dt: es.enter_context(nc.sbuf_tensor(name, shape, dt))
        ps = lambda name, shape, dt=F32: es.enter_context(nc.psum_tensor(name, shape, dt))
        b = B()
        b.idb = sb("idb_sb", [128, 128], BF16)
        PG = ps("PG", [128, 512]); PT = ps("PT", [128, 128], BF16); PC2 = ps("PC2", [128, 16]); PB = ps("PB", [128, 16])
        stage = sb("stage", [128, 2048], F32)
        pcs = sb("pcs", [128, 16], F32); cnt = sb("cnt", [128, 16], F32); alw = sb("alw", [128, 16], F32)
        onesf = sb("onesfs", [128, 1], F32); onesr = sb("onesrs", [128, 128], F32)
        lbb = sb("lbb", [128, D], BF16)
        x_sb = sb("x_sb", [128, D], F32); mn = sb("mn", [128, D], F32)
        wgt = sb("wgt", [128, 16], F32); wgt2 = sb("wgt2", [128, 16], F32); w8 = sb("w8", [128, 8], F32)
        tpt = sb("tpt", [128, 128], F32); tps = sb("tps", [128, 128], F32); tpsb = sb("tpsb", [128, 128], BF16)
        tpstb = sb("tpstb", [128, 128], BF16)
        ev = sb("ev", [128, 512], F32); osb = sb("osb", [128, D], F32)

        S = Seq()
        emit_consts(nc, es, S, LN_CONSTS)
        S.dma(b.idb[:, :], idb_h[:, :])
        S.dma(onesf[:, :], onesf_h[:, :], wait=False)
        S.dma(onesr[0:1, 0:128], onesr_h[0:1, :], wait=False)
        S.dma(pcs[0:8, 0:16], pcall[:, :], wait=False)
        S.op('tensor', lambda e: e.matmul(PC2[0:1, 0:16], onesf[0:8, 0:1], pcs[0:8, 0:16], start=True, stop=True))
        S.op('scalar', lambda e: e.copy(cnt[0:1, 0:16], PC2[0:1, 0:16]))
        S.op('tensor', lambda e: e.matmul(PB[:, :], onesr[0:1, 0:128], cnt[0:1, 0:16], start=True, stop=True))
        S.op('vector', lambda e: e.tensor_single_scalar(alw[:, :], PB[:, :], 3072.5, ALU.is_le))
        b.stage = stage
        for h in range(2):
            S.dma(stage[:, :], lBf[:, h * 2048:(h + 1) * 2048])
            S.op('vector', lambda e, h=h: e.tensor_single_scalar(
                lbb[:, h * 2048:(h + 1) * 2048], stage[:, :], 1.0 / 16, ALU.mult))

        for t in range(NTT):
            S.dma(wgt[:, :], wg[t * 128:(t + 1) * 128, :])
            S.dma(tpt[:, :], tmp[t * 128:(t + 1) * 128, :], wait=False)
            S.dma(x_sb[:, :], xn[t * 128:(t + 1) * 128, :], wait=False)
            S.dma(mn[:, :], main[t * 128:(t + 1) * 128, :], wait=False)
            S.op('vector', lambda e: e.tensor_mul(wgt2[:, :], wgt[:, :], alw[:, :]))
            S.op('vector', lambda e: e.tensor_add(w8[:, :], wgt2[:, 0:8], wgt2[:, 8:16]))
            for ex in range(8):
                S.op('vector', lambda e, ex=ex: e.tensor_single_scalar(
                    tps[:, ex * 16:(ex + 1) * 16], tpt[:, ex * 16:(ex + 1) * 16], w8[:, ex:ex + 1], ALU.mult))
            S.op('vector', lambda e: e.tensor_copy(tpsb[:, :], tps[:, :]))
            S.op('tensor', lambda e: e.transpose(PT[:, :], tpsb[:, :], b.idb[:, :]))
            S.op('scalar', lambda e: e.copy(tpstb[:, :], PT[:, :]))
            for oc in range(8):
                S.op('tensor', lambda e, oc=oc: e.matmul(PG[:, :], tpstb[:, :], lbb[:, oc * 512:(oc + 1) * 512],
                                                         start=True, stop=True))
                S.op('vector', lambda e, oc=oc: e.tensor_add(ev[:, :], PG[:, :], mn[:, oc * 512:(oc + 1) * 512]))
                S.op('vector', lambda e, oc=oc: e.scalar_tensor_tensor(
                    osb[:, oc * 512:(oc + 1) * 512], ev[:, :], 0.5, x_sb[:, oc * 512:(oc + 1) * 512], ALU.mult, ALU.add))
            S.dma(out[t * 128:(t + 1) * 128, :], osb[:, :])
        S.fin()
        _emit(nc, S, es)
    return nc


def build_p3a():
    nc = bass.Bass("TRN2", target_bir_lowering=False)
    xn = nc.dram_tensor("xn", [TOK, D], F32, kind="ExternalInput").ap()
    wqT = nc.dram_tensor("wqT", [D, D], F32, kind="ExternalInput").ap()
    wkT = nc.dram_tensor("wkT", [D, 1024], F32, kind="ExternalInput").ap()
    wvT = nc.dram_tensor("wvT", [D, 1024], F32, kind="ExternalInput").ap()
    bqb = nc.dram_tensor("bqb", [128, D], F32, kind="ExternalInput").ap()
    bkb = nc.dram_tensor("bkb", [128, 1024], F32, kind="ExternalInput").ap()
    bvb = nc.dram_tensor("bvb", [128, 1024], F32, kind="ExternalInput").ap()
    idb_h = nc.dram_tensor("idb", [128, 128], BF16, kind="ExternalInput").ap()
    onesb_h = nc.dram_tensor("onesb", [128, 1], BF16, kind="ExternalInput").ap()
    q_o = nc.dram_tensor("q", [TOK, D], F32, kind="ExternalOutput").ap()
    Mp = nc.dram_tensor("Mp", [1024, 128], F32, kind="ExternalOutput").ap()
    ksp = nc.dram_tensor("ksp", [1, 1024], F32, kind="ExternalOutput").ap()

    es = contextlib.ExitStack()
    with es:
        b = B()
        sb = _alloc_common(nc, es, b)
        ps = lambda name, shape, dt=F32: es.enter_context(nc.psum_tensor(name, shape, dt))
        b.PG = ps("PG", [128, 512]); b.PT = ps("PT", [128, 128], BF16)
        PM = ps("PM", [128, 128]); PR = ps("PR", [128, 512])
        k_sb = sb("k_sb", [128, NTT * 1024], BF16)
        v_sb = sb("v_sb", [128, NTT * 1024], BF16)
        onesb = sb("onesbs", [128, 1], BF16)
        bias_c = sb("bias_c", [128, 512], F32)
        qs = sb("qs", [128, 512], F32); rb = sb("rb", [128, 512], F32)
        eb = sb("eb", [128, 512], F32)

        S = Seq()
        emit_consts(nc, es, S, LN_CONSTS)
        S.dma(b.idb[:, :], idb_h[:, :])
        S.dma(onesb[:, :], onesb_h[:, :], wait=False)
        emit_ln_phase(S, xn, b)

        def phi_post(bias_hbm, out_hbm, store_sb):
            def pre_oc(oc):
                S.dma(bias_c[:, :], bias_hbm[:, oc * 512:(oc + 1) * 512])

            def post(tt, oc):
                S.op('vector', lambda e: e.tensor_add(qs[:, :], b.PG[:, :], bias_c[:, :]))
                S.op('scalar', lambda e: e.activation(rb[:, :], qs[:, :], AF.Relu))
                S.op('vector', lambda e: e.tensor_sub(eb[:, :], qs[:, :], rb[:, :]))
                S.op('scalar', lambda e: e.activation(qs[:, :], eb[:, :], AF.Exp))
                S.op('vector', lambda e: e.tensor_add(eb[:, :], rb[:, :], qs[:, :]))
                if out_hbm is not None:
                    S.dma(out_hbm[tt * 128:(tt + 1) * 128, oc * 512:(oc + 1) * 512], eb[:, :])
                if store_sb is not None:
                    S.op('vector', lambda e, tt=tt, oc=oc: e.tensor_copy(
                        store_sb[:, tt * 1024 + oc * 512: tt * 1024 + oc * 512 + 512], eb[:, :]))
            return pre_oc, post

        pq, po = phi_post(bqb, q_o, None)
        emit_gemm(S, wqT, 8, b, lambda tt, dt: b.lnT[:, dt * 1024 + tt * 128: dt * 1024 + tt * 128 + 128],
                  pre_oc=pq, post=po)
        pq, po = phi_post(bkb, None, k_sb)
        emit_gemm(S, wkT, 2, b, lambda tt, dt: b.lnT[:, dt * 1024 + tt * 128: dt * 1024 + tt * 128 + 128],
                  pre_oc=pq, post=po)

        def pre_v(oc):
            S.dma(bias_c[:, :], bvb[:, oc * 512:(oc + 1) * 512])

        def post_v(tt, oc):
            S.op('vector', lambda e: e.tensor_add(qs[:, :], b.PG[:, :], bias_c[:, :]))
            S.op('vector', lambda e, tt=tt, oc=oc: e.tensor_copy(
                v_sb[:, tt * 1024 + oc * 512: tt * 1024 + oc * 512 + 512], qs[:, :]))
        emit_gemm(S, wvT, 2, b, lambda tt, dt: b.lnT[:, dt * 1024 + tt * 128: dt * 1024 + tt * 128 + 128],
                  pre_oc=pre_v, post=post_v)

        for kvh in range(8):
            for tt in range(NTT):
                S.op('tensor', lambda e, kvh=kvh, tt=tt: e.matmul(
                    PM[:, :], k_sb[:, tt * 1024 + kvh * 128: tt * 1024 + kvh * 128 + 128],
                    v_sb[:, tt * 1024 + kvh * 128: tt * 1024 + kvh * 128 + 128],
                    start=(tt == 0), stop=(tt == NTT - 1)), wait=(tt == 0))
            S.op('scalar', lambda e: e.copy(rb[:, 0:128], PM[:, :]))
            S.dma(Mp[kvh * 128:(kvh + 1) * 128, :], rb[:, 0:128])
        for half in range(2):
            for tt in range(NTT):
                S.op('tensor', lambda e, half=half, tt=tt: e.matmul(
                    PR[0:1, 0:512], onesb[:, 0:1], k_sb[:, tt * 1024 + half * 512: tt * 1024 + half * 512 + 512],
                    start=(tt == 0), stop=(tt == NTT - 1)), wait=(tt == 0))
            S.op('scalar', lambda e: e.copy(qs[0:1, 0:512], PR[0:1, 0:512]))
            S.dma(ksp[0:1, half * 512:(half + 1) * 512], qs[0:1, 0:512])
        S.fin()
        _emit(nc, S, es)
    return nc


def build_p3b():
    nc = bass.Bass("TRN2", target_bir_lowering=False)
    qT = nc.dram_tensor("qT", [D, TOK], F32, kind="ExternalInput").ap()
    xn = nc.dram_tensor("xn", [TOK, D], F32, kind="ExternalInput").ap()
    Mall = nc.dram_tensor("Mall", [4096, 128], F32, kind="ExternalInput").ap()
    ksall = nc.dram_tensor("ksall", [4, 1024], F32, kind="ExternalInput").ap()
    woT = nc.dram_tensor("woT", [D, D], F32, kind="ExternalInput").ap()
    bob = nc.dram_tensor("bob", [128, D], F32, kind="ExternalInput").ap()
    idb_h = nc.dram_tensor("idb", [128, 128], BF16, kind="ExternalInput").ap()
    idf_h = nc.dram_tensor("idf", [128, 128], F32, kind="ExternalInput").ap()
    onesf_h = nc.dram_tensor("onesf", [128, 1], F32, kind="ExternalInput").ap()
    out = nc.dram_tensor("out", [TOK, D], F32, kind="ExternalOutput").ap()

    es = contextlib.ExitStack()
    with es:
        sb = lambda name, shape, dt: es.enter_context(nc.sbuf_tensor(name, shape, dt))
        ps = lambda name, shape, dt=F32: es.enter_context(nc.psum_tensor(name, shape, dt))
        b = B()
        b.idb = sb("idb_sb", [128, 128], BF16)
        idf = sb("idf_sb", [128, 128], F32)
        b.stage = sb("stage", [128, 2048], F32)
        b.wtb = sb("wtb", [128, NKD * 512], BF16)
        b.PG = ps("PG", [128, 512]); PA = ps("PA", [128, 512]); PR = ps("PR", [128, 512]); PTq = ps("PTq", [128, 128])
        attb = sb("attb", [128, 32 * 1024], BF16)
        Mgb = sb("Mgb", [128, 1024], BF16)
        mtmp = sb("mtmp", [128, 512], F32); ma = sb("ma", [128, 128], F32)
        mb_ = sb("mb_", [128, 128], F32); mc = sb("mc", [128, 128], F32)
        ks4 = sb("ks4", [128, 1024], F32); ksr = sb("ksr", [128, 1024], F32); ksr2 = sb("ksr2", [128, 1024], F32)
        rk = sb("rk", [128, 8], F32); ev1 = sb("ev1", [128, 1], F32)
        onesf = sb("onesfs", [128, 1], F32)
        qtb = sb("qtb", [128, 1024], BF16); qstage = sb("qstage", [128, 1024], F32)
        bias_c = sb("bias_c", [128, 512], F32); xq = sb("xq", [128, 512], F32)
        ev = sb("ev", [128, 512], F32); ev2 = sb("ev2", [128, 512], F32)

        S = Seq()
        emit_consts(nc, es, S, LN_CONSTS)
        S.dma(b.idb[:, :], idb_h[:, :])
        S.dma(idf[:, :], idf_h[:, :], wait=False)
        S.dma(onesf[:, :], onesf_h[:, :], wait=False)
        for kvh in range(8):
            for j in range(4):
                S.dma(mtmp[:, j * 128:(j + 1) * 128], Mall[j * 1024 + kvh * 128: j * 1024 + kvh * 128 + 128, :],
                      wait=(j == 0))
            S.op('vector', lambda e: e.tensor_add(ma[:, :], mtmp[:, 0:128], mtmp[:, 128:256]))
            S.op('vector', lambda e: e.tensor_add(mb_[:, :], mtmp[:, 256:384], mtmp[:, 384:512]))
            S.op('vector', lambda e: e.tensor_add(mc[:, :], ma[:, :], mb_[:, :]))
            S.op('vector', lambda e, kvh=kvh: e.tensor_copy(Mgb[:, kvh * 128:(kvh + 1) * 128], mc[:, :]))
        S.dma(ks4[0:4, 0:1024], ksall[:, :])
        for half in range(2):
            S.op('tensor', lambda e, half=half: e.matmul(
                PR[0:1, 0:512], onesf[0:4, 0:1], ks4[0:4, half * 512:(half + 1) * 512], start=True, stop=True))
            S.op('scalar', lambda e, half=half: e.copy(ksr[0:1, half * 512:(half + 1) * 512], PR[0:1, 0:512]))
        S.op('scalar', lambda e: e.add(ksr2[0:1, 0:1024], ksr[0:1, 0:1024], 1e-10))
        for kvh in range(8):
            S.op('tensor', lambda e, kvh=kvh: e.transpose(PTq[:, 0:1], ksr2[0:1, kvh * 128:(kvh + 1) * 128], idf[0:1, 0:1]))
            S.op('scalar', lambda e: e.copy(ev1[:, 0:1], PTq[:, 0:1]))
            S.op('vector', lambda e, kvh=kvh: e.reciprocal(rk[:, kvh:kvh + 1], ev1[:, 0:1]))
        for qh in range(32):
            kvh = qh // 4
            S.dma(qstage[:, :], qT[qh * 128:(qh + 1) * 128, :])
            S.op('vector', lambda e: e.tensor_copy(qtb[:, :], qstage[:, :]))
            for half in range(2):
                S.op('tensor', lambda e, kvh=kvh, half=half: e.matmul(
                    PA[:, :], Mgb[:, kvh * 128:(kvh + 1) * 128], qtb[:, half * 512:(half + 1) * 512],
                    start=True, stop=True))
                S.op('vector', lambda e, qh=qh, kvh=kvh, half=half: e.tensor_single_scalar(
                    attb[:, qh * 1024 + half * 512: qh * 1024 + half * 512 + 512], PA[:, :],
                    rk[:, kvh:kvh + 1], ALU.mult))

        def pre_oc(oc):
            S.dma(bias_c[:, :], bob[:, oc * 512:(oc + 1) * 512])

        def post(tt, oc):
            S.dma(xq[:, :], xn[tt * 128:(tt + 1) * 128, oc * 512:(oc + 1) * 512])
            S.op('vector', lambda e: e.tensor_add(ev[:, :], b.PG[:, :], bias_c[:, :]))
            S.op('vector', lambda e: e.tensor_add(ev2[:, :], ev[:, :], xq[:, :]))
            S.dma(out[tt * 128:(tt + 1) * 128, oc * 512:(oc + 1) * 512], ev2[:, :])

        emit_gemm(S, woT, 8, b, lambda tt, dt: attb[:, dt * 1024 + tt * 128: dt * 1024 + tt * 128 + 128],
                  pre_oc=pre_oc, post=post)
        S.fin()
        _emit(nc, S, es)
    return nc


def build_p4a():
    nc = bass.Bass("TRN2", target_bir_lowering=False)
    xn = nc.dram_tensor("xn", [TOK, D], F32, kind="ExternalInput").ap()
    wqT = nc.dram_tensor("wqT", [D, D], F32, kind="ExternalInput").ap()
    wkT = nc.dram_tensor("wkT", [D, D], F32, kind="ExternalInput").ap()
    wvT = nc.dram_tensor("wvT", [D, D], F32, kind="ExternalInput").ap()
    cosc = nc.dram_tensor("cosc", [TOK, 128], F32, kind="ExternalInput").ap()
    sinc = nc.dram_tensor("sinc", [TOK, 128], F32, kind="ExternalInput").ap()
    idb_h = nc.dram_tensor("idb", [128, 128], BF16, kind="ExternalInput").ap()
    q_o = nc.dram_tensor("q", [TOK, D], F32, kind="ExternalOutput").ap()
    k_o = nc.dram_tensor("k", [TOK, D], F32, kind="ExternalOutput").ap()
    v_o = nc.dram_tensor("v", [TOK, D], F32, kind="ExternalOutput").ap()

    es = contextlib.ExitStack()
    with es:
        b = B()
        sb = _alloc_common(nc, es, b)
        ps = lambda name, shape, dt=F32: es.enter_context(nc.psum_tensor(name, shape, dt))
        b.PG = ps("PG", [128, 512]); b.PT = ps("PT", [128, 128], BF16)
        cos_sb = sb("cos_sb", [128, 1024], F32); sin_sb = sb("sin_sb", [128, 1024], F32)
        qs = sb("qs", [128, 512], F32); ro = sb("ro", [128, 512], F32)
        u1 = sb("u1", [128, 64], F32); u2 = sb("u2", [128, 64], F32)

        S = Seq()
        emit_consts(nc, es, S, LN_CONSTS)
        S.dma(b.idb[:, :], idb_h[:, :])
        for tt in range(NTT):
            S.dma(cos_sb[:, tt * 128:(tt + 1) * 128], cosc[tt * 128:(tt + 1) * 128, :], wait=(tt == 0))
        for tt in range(NTT):
            S.dma(sin_sb[:, tt * 128:(tt + 1) * 128], sinc[tt * 128:(tt + 1) * 128, :], wait=False)
        emit_ln_phase(S, xn, b)

        def rope_post(out_hbm):
            def post(tt, oc):
                S.op('scalar', lambda e: e.copy(qs[:, :], b.PG[:, :]))
                for h4 in range(4):
                    base = h4 * 128
                    c1 = cos_sb[:, tt * 128: tt * 128 + 64]
                    c2 = cos_sb[:, tt * 128 + 64: tt * 128 + 128]
                    s1 = sin_sb[:, tt * 128: tt * 128 + 64]
                    s2 = sin_sb[:, tt * 128 + 64: tt * 128 + 128]
                    S.op('vector', lambda e, base=base, s1=s1: e.tensor_mul(u1[:, :], qs[:, base + 64:base + 128], s1))
                    S.op('vector', lambda e, base=base, c1=c1: e.tensor_mul(u2[:, :], qs[:, base:base + 64], c1))
                    S.op('vector', lambda e, base=base: e.tensor_sub(ro[:, base:base + 64], u2[:, :], u1[:, :]))
                    S.op('vector', lambda e, base=base, s2=s2: e.tensor_mul(u1[:, :], qs[:, base:base + 64], s2))
                    S.op('vector', lambda e, base=base, c2=c2: e.tensor_mul(u2[:, :], qs[:, base + 64:base + 128], c2))
                    S.op('vector', lambda e, base=base: e.tensor_add(ro[:, base + 64:base + 128], u2[:, :], u1[:, :]))
                S.dma(out_hbm[tt * 128:(tt + 1) * 128, oc * 512:(oc + 1) * 512], ro[:, :])
            return post

        lhsT = lambda tt, dt: b.lnT[:, dt * 1024 + tt * 128: dt * 1024 + tt * 128 + 128]
        emit_gemm(S, wqT, 8, b, lhsT, post=rope_post(q_o))
        emit_gemm(S, wkT, 8, b, lhsT, post=rope_post(k_o))

        def post_v(tt, oc):
            S.op('scalar', lambda e: e.copy(qs[:, :], b.PG[:, :]))
            S.dma(v_o[tt * 128:(tt + 1) * 128, oc * 512:(oc + 1) * 512], qs[:, :])
        emit_gemm(S, wvT, 8, b, lhsT, post=post_v)
        S.fin()
        _emit(nc, S, es)
    return nc


def build_p4b():
    nc = bass.Bass("TRN2", target_bir_lowering=False)
    qTp = nc.dram_tensor("qTp", [1024, 4096], F32, kind="ExternalInput").ap()
    kTp = nc.dram_tensor("kTp", [1024, 4096], F32, kind="ExternalInput").ap()
    vp = nc.dram_tensor("vp", [8 * 4096, 128], F32, kind="ExternalInput").ap()
    idf_h = nc.dram_tensor("idf", [128, 128], F32, kind="ExternalInput").ap()
    onesb_h = nc.dram_tensor("onesb", [128, 1], BF16, kind="ExternalInput").ap()
    Op = nc.dram_tensor("Op", [8 * 4096, 128], F32, kind="ExternalOutput").ap()

    es = contextlib.ExitStack()
    with es:
        sb = lambda name, shape, dt: es.enter_context(nc.sbuf_tensor(name, shape, dt))
        ps = lambda name, shape, dt=F32: es.enter_context(nc.psum_tensor(name, shape, dt))
        PSa = ps("PSa", [128, 512]); PSb = ps("PSb", [128, 512])
        POa = ps("POa", [128, 512]); POb = ps("POb", [128, 512])
        PD0 = ps("PD0", [128, 512]); PD1 = ps("PD1", [128, 512])
        PTq = ps("PTq", [128, 128])
        idf = sb("idf_sb", [128, 128], F32)
        onesb = sb("onesbs", [128, 1], BF16)
        stage = sb("stage", [128, 2048], F32)
        ktb = sb("ktb", [128, 4096], BF16); qtb = sb("qtb", [128, 4096], BF16)
        vsb = sb("vsb", [128, 4096], BF16)
        expb = sb("expb", [128, 1024], BF16)
        dn = sb("dn", [128, 1024], F32)
        d1 = sb("d1", [128, 1], F32); d2 = sb("d2", [128, 1], F32)
        oev = sb("oev", [128, 128], F32)
        SCALE = 1.0 / float(np.sqrt(128.0))

        S = Seq()
        emit_consts(nc, es, S, LN_CONSTS)
        S.dma(idf[:, :], idf_h[:, :])
        S.dma(onesb[:, :], onesb_h[:, :], wait=False)
        for p in range(8):
            for h in range(2):
                S.dma(stage[:, :], kTp[p * 128:(p + 1) * 128, h * 2048:(h + 1) * 2048])
                S.op('vector', lambda e, h=h: e.tensor_copy(ktb[:, h * 2048:(h + 1) * 2048], stage[:, :]))
            for h in range(2):
                S.dma(stage[:, :], qTp[p * 128:(p + 1) * 128, h * 2048:(h + 1) * 2048])
                S.op('vector', lambda e, h=h: e.tensor_copy(qtb[:, h * 2048:(h + 1) * 2048], stage[:, :]))
            for h in range(2):
                for j in range(16):
                    kt = h * 16 + j
                    S.dma(stage[:, j * 128:(j + 1) * 128],
                          vp[p * 4096 + kt * 128: p * 4096 + kt * 128 + 128, :], wait=(j == 0))
                S.op('vector', lambda e, h=h: e.tensor_copy(vsb[:, h * 2048:(h + 1) * 2048], stage[:, :]))

            for qc in range(4):
                for kt in range(32):
                    S.op('tensor', lambda e, kt=kt, qc=qc: e.matmul(
                        PSa[:, :], ktb[:, kt * 128:(kt + 1) * 128], qtb[:, qc * 1024: qc * 1024 + 512],
                        start=True, stop=True))
                    S.op('tensor', lambda e, kt=kt, qc=qc: e.matmul(
                        PSb[:, :], ktb[:, kt * 128:(kt + 1) * 128], qtb[:, qc * 1024 + 512: qc * 1024 + 1024],
                        start=True, stop=True), wait=False)
                    S.op('scalar', lambda e: e.activation(expb[:, 0:512], PSa[:, :], AF.Exp, scale=SCALE))
                    S.op('scalar', lambda e: e.activation(expb[:, 512:1024], PSb[:, :], AF.Exp, scale=SCALE), wait=False)
                    S.op('tensor', lambda e, kt=kt: e.matmul(
                        PD0[0:1, 0:512], onesb[:, 0:1], expb[:, 0:512],
                        start=(kt == 0), stop=(kt == 31), skip_group_check=True))
                    S.op('tensor', lambda e, kt=kt: e.matmul(
                        PD1[0:1, 0:512], onesb[:, 0:1], expb[:, 512:1024],
                        start=(kt == 0), stop=(kt == 31), skip_group_check=True), wait=False)
                    for qt in range(8):
                        PO = POa if qt < 4 else POb
                        co = (qt % 4) * 128
                        S.op('tensor', lambda e, kt=kt, qt=qt, PO=PO, co=co: e.matmul(
                            PO[:, co:co + 128], expb[:, qt * 128:(qt + 1) * 128], vsb[:, kt * 128:(kt + 1) * 128],
                            start=(kt == 0), stop=(kt == 31), skip_group_check=True), wait=False)
                S.op('vector', lambda e: e.tensor_copy(dn[0:1, 0:512], PD0[0:1, 0:512]))
                S.op('vector', lambda e: e.tensor_copy(dn[0:1, 512:1024], PD1[0:1, 0:512]))
                for qt in range(8):
                    PO = POa if qt < 4 else POb
                    co = (qt % 4) * 128
                    S.op('tensor', lambda e, qt=qt: e.transpose(PTq[:, 0:1], dn[0:1, qt * 128:(qt + 1) * 128], idf[0:1, 0:1]))
                    S.op('scalar', lambda e: e.copy(d1[:, 0:1], PTq[:, 0:1]))
                    S.op('vector', lambda e: e.reciprocal(d2[:, 0:1], d1[:, 0:1]))
                    S.op('vector', lambda e, PO=PO, co=co: e.tensor_single_scalar(oev[:, :], PO[:, co:co + 128], d2[:, 0:1], ALU.mult))
                    S.dma(Op[p * 4096 + qc * 1024 + qt * 128: p * 4096 + qc * 1024 + qt * 128 + 128, :], oev[:, :])
        S.fin()
        _emit(nc, S, es)
    return nc


def build_p4c():
    nc = bass.Bass("TRN2", target_bir_lowering=False)
    attnT = nc.dram_tensor("attnT", [D, TOK], F32, kind="ExternalInput").ap()
    xn = nc.dram_tensor("xn", [TOK, D], F32, kind="ExternalInput").ap()
    woT = nc.dram_tensor("woT", [D, D], F32, kind="ExternalInput").ap()
    out = nc.dram_tensor("out", [TOK, D], F32, kind="ExternalOutput").ap()

    es = contextlib.ExitStack()
    with es:
        sb = lambda name, shape, dt: es.enter_context(nc.sbuf_tensor(name, shape, dt))
        ps = lambda name, shape, dt=F32: es.enter_context(nc.psum_tensor(name, shape, dt))
        b = B()
        b.stage = sb("stage", [128, 2048], F32)
        b.wtb = sb("wtb", [128, NKD * 512], BF16)
        b.PG = ps("PG", [128, 512])
        attb = sb("attb", [128, NKD * 1024], BF16)
        xq = sb("xq", [128, 512], F32); ev = sb("ev", [128, 512], F32)

        S = Seq()
        emit_consts(nc, es, S, LN_CONSTS)
        load_conv(S, b, attnT, attb, NKD * 1024, 1024)

        def post(tt, oc):
            S.dma(xq[:, :], xn[tt * 128:(tt + 1) * 128, oc * 512:(oc + 1) * 512])
            S.op('vector', lambda e: e.tensor_add(ev[:, :], b.PG[:, :], xq[:, :]))
            S.dma(out[tt * 128:(tt + 1) * 128, oc * 512:(oc + 1) * 512], ev[:, :])

        emit_gemm(S, woT, 8, b, lambda tt, dt: attb[:, dt * 1024 + tt * 128: dt * 1024 + tt * 128 + 128],
                  post=post)
        S.fin()
        _emit(nc, S, es)
    return nc


def build_p5():
    nc = bass.Bass("TRN2", target_bir_lowering=False)
    xn = nc.dram_tensor("xn", [TOK, D], F32, kind="ExternalInput").ap()
    wT = nc.dram_tensor("wT", [D, D], F32, kind="ExternalInput").ap()
    scb = nc.dram_tensor("scb", [128, 1024], F32, kind="ExternalInput").ap()
    idb_h = nc.dram_tensor("idb", [128, 128], BF16, kind="ExternalInput").ap()
    out = nc.dram_tensor("out", [TOK, D], F32, kind="ExternalOutput").ap()

    es = contextlib.ExitStack()
    with es:
        b = B()
        sb = _alloc_common(nc, es, b)
        ps = lambda name, shape, dt=F32: es.enter_context(nc.psum_tensor(name, shape, dt))
        b.PG = ps("PG", [128, 512]); b.PT = ps("PT", [128, 128], BF16)
        b.sc = sb("sc_sb", [128, 1024], F32)
        xq = sb("xq", [128, 512], F32)
        ev5 = sb("ev5", [128, 512], F32)

        S = Seq()
        emit_consts(nc, es, S, LN_CONSTS)
        S.dma(b.idb[:, :], idb_h[:, :])
        S.dma(b.sc[:, :], scb[:, :], wait=False)
        emit_ln_phase(S, xn, b)

        def post(tt, oc):
            S.dma(xq[:, :], xn[tt * 128:(tt + 1) * 128, oc * 512:(oc + 1) * 512])
            S.op('vector', lambda e: e.scalar_tensor_tensor(
                ev5[:, :], b.PG[:, :], 0.5, xq[:, :], ALU.mult, ALU.add))
            S.dma(out[tt * 128:(tt + 1) * 128, oc * 512:(oc + 1) * 512], ev5[:, :])

        emit_gemm(S, wT, 8, b, lambda tt, dt: b.lnT[:, dt * 1024 + tt * 128: dt * 1024 + tt * 128 + 128],
                  post=post, dequant=True)
        S.fin()
        _emit(nc, S, es)
    return nc


_CACHE = {}


def _prog(name, builder):
    if name not in _CACHE:
        _CACHE[name] = builder()
    return _CACHE[name]


def _run(nc, maps):
    res = run_bass_kernel_spmd(nc, maps, core_ids=list(range(8)))
    return res.results


def _T(a):
    return np.ascontiguousarray(np.asarray(a, np.float32).T)


def _bc(v):
    v = np.asarray(v, np.float32).reshape(-1)
    return np.ascontiguousarray(np.broadcast_to(v[None, :], (128, v.shape[0])))


def kernel(**inp):
    f32 = lambda k: np.asarray(inp[k], np.float32)
    x = np.ascontiguousarray(f32("L0_attn_hidden_states").reshape(8192, 4096))

    idb = np.eye(128, dtype=BFDT)
    idf = np.eye(128, dtype=np.float32)
    onesf = np.ones((128, 1), np.float32)
    onesr = np.ones((1, 128), np.float32)
    onesb = np.ones((128, 1), BFDT)

    wT0 = _T(f32("L0_ffn_weight"))
    gw1T = _T(f32("L0_ffn_gw1"))
    gw2T = _T(f32("L0_ffn_gw2"))
    gb1b = _bc(f32("L0_ffn_gb1")); gamb = _bc(f32("L0_ffn_ln_gamma"))
    betb = _bc(f32("L0_ffn_ln_beta")); gb2b = _bc(f32("L0_ffn_gb2"))
    lAf = np.ascontiguousarray(np.transpose(f32("L0_ffn_lora_A"), (1, 0, 2)).reshape(4096, 128))
    lBf = np.ascontiguousarray(f32("L0_ffn_lora_B").reshape(128, 4096))
    wqT0 = _T(f32("L0_attn_Wq")); wkT0 = _T(f32("L0_attn_Wk")); wvT0 = _T(f32("L0_attn_Wv"))
    woT0 = _T(f32("L0_attn_Wo"))
    bqb = _bc(f32("L0_attn_bq")); bkb = _bc(f32("L0_attn_bk")); bvb = _bc(f32("L0_attn_bv"))
    bob = _bc(f32("L0_attn_bo"))
    wqT1 = _T(f32("L1_attn_Wq")); wkT1 = _T(f32("L1_attn_Wk")); wvT1 = _T(f32("L1_attn_Wv"))
    woT1 = _T(f32("L1_attn_Wo"))
    wT1 = _T(f32("L1_ffn_weight"))
    sc = f32("L1_ffn_weight_scale")
    sc_host = np.empty((128, 1024), np.float32)
    for dt in range(32):
        for g in range(32):
            sc_host[:, dt * 32 + g] = sc[g, dt]
    inv = 1.0 / (10000.0 ** (np.arange(0, 128, 2, dtype=np.float32) / 128.0))
    t = np.arange(4096, dtype=np.float32)
    freqs = t[:, None] * inv[None, :]
    emb = np.concatenate([freqs, freqs], axis=-1)
    cos_t = np.cos(emb).astype(np.float32)
    sin_t = np.sin(emb).astype(np.float32)

    P1 = _prog("p1", build_p1); P2 = _prog("p2", build_p2)
    P3a = _prog("p3a", build_p3a); P3b = _prog("p3b", build_p3b)
    P4a = _prog("p4a", build_p4a); P4b = _prog("p4b", build_p4b)
    P4c = _prog("p4c", build_p4c); P5 = _prog("p5", build_p5)

    def shards(xf):
        return [np.ascontiguousarray(xf[c * 1024:(c + 1) * 1024]) for c in range(8)]

    def ffn0(xf):
        xs = shards(xf)
        r1 = _run(P1, [dict(xn=xs[c], wT=wT0, gw1T=gw1T, gw2T=gw2T, gb1b=gb1b, gamb=gamb,
                            betb=betb, gb2b=gb2b, lAf=lAf, idb=idb, onesf=onesf) for c in range(8)])
        pcall = np.ascontiguousarray(np.concatenate([r1[c]["pc"] for c in range(8)], axis=0))
        r2 = _run(P2, [dict(xn=xs[c], main=r1[c]["main"], tmp=r1[c]["tmp"], wg=r1[c]["wg"],
                            pcall=pcall, lBf=lBf, idb=idb, onesf=onesf, onesr=onesr) for c in range(8)])
        return np.concatenate([r2[c]["out"] for c in range(8)], axis=0)

    def attn0(xf):
        xs = shards(xf)
        r1 = _run(P3a, [dict(xn=xs[c], wqT=wqT0, wkT=wkT0, wvT=wvT0, bqb=bqb, bkb=bkb, bvb=bvb,
                             idb=idb, onesb=onesb) for c in range(8)])
        maps = []
        for c in range(8):
            bb = c // 4
            Mall = np.ascontiguousarray(np.concatenate([r1[4 * bb + j]["Mp"] for j in range(4)], axis=0))
            ksall = np.ascontiguousarray(np.concatenate([r1[4 * bb + j]["ksp"] for j in range(4)], axis=0))
            maps.append(dict(qT=_T(r1[c]["q"]), xn=xs[c], Mall=Mall, ksall=ksall, woT=woT0,
                             bob=bob, idb=idb, idf=idf, onesf=onesf))
        r2 = _run(P3b, maps)
        return np.concatenate([r2[c]["out"] for c in range(8)], axis=0)

    def ffn1(xf):
        xs = shards(xf)
        r = _run(P5, [dict(xn=xs[c], wT=wT1, scb=sc_host, idb=idb) for c in range(8)])
        return np.concatenate([r[c]["out"] for c in range(8)], axis=0)

    def attn1(xf):
        xs = shards(xf)
        r1 = _run(P4a, [dict(xn=xs[c], wqT=wqT1, wkT=wkT1, wvT=wvT1,
                             cosc=np.ascontiguousarray(cos_t[(c % 4) * 1024:(c % 4 + 1) * 1024]),
                             sinc=np.ascontiguousarray(sin_t[(c % 4) * 1024:(c % 4 + 1) * 1024]),
                             idb=idb) for c in range(8)])
        Q = np.concatenate([r1[c]["q"] for c in range(8)], axis=0).reshape(2, 4096, 4096)
        K = np.concatenate([r1[c]["k"] for c in range(8)], axis=0).reshape(2, 4096, 4096)
        V = np.concatenate([r1[c]["v"] for c in range(8)], axis=0).reshape(2, 4096, 4096)
        maps = []
        for c in range(8):
            qT_l, kT_l, v_l = [], [], []
            for j in range(8):
                f = c * 8 + j
                bb, h = f // 32, f % 32
                qT_l.append(_T(Q[bb, :, h * 128:(h + 1) * 128]))
                kT_l.append(_T(K[bb, :, h * 128:(h + 1) * 128]))
                v_l.append(np.ascontiguousarray(V[bb, :, h * 128:(h + 1) * 128]))
            maps.append(dict(qTp=np.ascontiguousarray(np.concatenate(qT_l, 0)),
                             kTp=np.ascontiguousarray(np.concatenate(kT_l, 0)),
                             vp=np.ascontiguousarray(np.concatenate(v_l, 0)),
                             idf=idf, onesb=onesb))
        r2 = _run(P4b, maps)
        A = np.empty((2, 4096, 4096), np.float32)
        for c in range(8):
            for j in range(8):
                f = c * 8 + j
                bb, h = f // 32, f % 32
                A[bb, :, h * 128:(h + 1) * 128] = r2[c]["Op"][j * 4096:(j + 1) * 4096, :]
        Af = A.reshape(8192, 4096)
        maps = [dict(attnT=_T(Af[c * 1024:(c + 1) * 1024]), xn=xs[c], woT=woT1) for c in range(8)]
        r3 = _run(P4c, maps)
        return np.concatenate([r3[c]["out"] for c in range(8)], axis=0)

    x = ffn0(x)
    x = attn0(x)
    x = ffn0(x)
    x = ffn1(x)
    x = attn1(x)
    x = ffn1(x)
    return x.reshape(2, 4096, 4096)


if __name__ == "__main__":
    import reference as R
    inputs = R.setup_inputs(0)
    out = kernel(**{k: np.asarray(v) for k, v in inputs.items()})
    ref = np.asarray(R.reference(**inputs))
    err = np.sqrt(np.mean((out - ref) ** 2) / np.mean(ref ** 2))
    print("rel l2:", err)


# version 02 (final)
# speedup vs baseline: 1.2647x; 1.2647x over previous
import contextlib
import numpy as np

import concourse.bass as bass
import concourse.mybir as mybir
from concourse.bass_utils import run_bass_kernel_spmd

try:
    import ml_dtypes
    BFDT = ml_dtypes.bfloat16
except Exception:
    import jax.numpy as jnp
    BFDT = jnp.bfloat16

F32 = mybir.dt.float32
BF16 = mybir.dt.bfloat16
AF = mybir.ActivationFunctionType
ALU = mybir.AluOpType
AX = mybir.AxisListType.X

ENGINES = ("sync", "tensor", "vector", "scalar")

D = 4096
TOK = 1024
NTT = 8
NKD = 32


class Seq:

    def __init__(self):
        self.items = {e: [] for e in ENGINES}
        self.cum = {e: 0 for e in ENGINES}
        self.last = None

    def op(self, eng, fn, inc=1, wait=True, extra=()):
        if wait and self.last is not None:
            self.items[eng].append(("w", self.last[0], self.last[1]))
        for (weng, wval) in extra:
            self.items[eng].append(("w", weng, wval))
        self.cum[eng] += inc
        self.items[eng].append(("i", fn, inc))
        self.last = (eng, self.cum[eng])

    def dma(self, out, in_, wait=True):
        self.op("sync", lambda e, o=out, i=in_: e.dma_start(o, i), inc=16, wait=wait)

    def fin(self):
        self.items["scalar"].append(("w", self.last[0], self.last[1]))


def _emit(nc, S, es):
    sems = {e: es.enter_context(nc.semaphore(name="sem_" + e)) for e in ENGINES}
    with nc.Block() as block:
        for ename in ENGINES:
            items = S.items[ename]
            if not items:
                continue

            def make(items=items, ename=ename):
                def _(eng):
                    for it in items:
                        if it[0] == "w":
                            eng.wait_ge(sems[it[1]], it[2])
                        else:
                            ins = it[1](eng)
                            ins.then_inc(sems[ename], it[2])
                return _
            getattr(block, ename)(make())


class B:
    pass


def _alloc_common(nc, es, bufs, with_lnT=True):
    sb = lambda name, shape, dt: es.enter_context(nc.sbuf_tensor(name, shape, dt))
    bufs.x = sb("x_sb", [128, 4096], F32)
    bufs.xc = sb("xc_sb", [128, 4096], F32)
    bufs.lnb = sb("lnb_sb", [128, 4096], BF16)
    if with_lnT:
        bufs.lnT = sb("lnT_sb", [128, NKD * 1024], BF16)
    bufs.mu = sb("mu_sb", [128, 4], F32)
    bufs.stage = sb("stage_sb", [128, 2048], F32)
    bufs.wtb = sb("wtb_sb", [128, NKD * 512], BF16)
    bufs.idb = sb("idb_sb", [128, 128], BF16)
    return sb


def emit_consts(nc, es, S, vals):
    cp = es.enter_context(nc.sbuf_tensor("cpool", [128, max(1, len(vals))], F32))
    for i, v in enumerate(vals):
        S.op('vector', lambda e, i=i, v=v: e.memset(cp[:, i:i + 1], v), wait=(i == 0))
        nc.const_aps.register(F32, v, cp[:, i:i + 1])


LN_CONSTS = (1e-5, 1.0 / 4096, 1.0 / 32, 1e-10, 1.0 / float(np.sqrt(128.0)))


def emit_ln_phase(S, xn_ap, b, per_t=None):
    for t in range(NTT):
        S.dma(b.x[:, :], xn_ap[t * 128:(t + 1) * 128, :])
        S.op('vector', lambda e, b=b: e.reduce_sum(b.mu[:, 0:1], b.x[:, :], AX))
        S.op('scalar', lambda e, b=b: e.mul(b.mu[:, 1:2], b.mu[:, 0:1], 1.0 / 4096))
        S.op('vector', lambda e, b=b: e.tensor_single_scalar(b.xc[:, :], b.x[:, :], b.mu[:, 1:2], ALU.subtract))
        S.op('scalar', lambda e, b=b: e.activation(b.x[:, :], b.xc[:, :], AF.Square))
        S.op('vector', lambda e, b=b: e.reduce_sum(b.mu[:, 2:3], b.x[:, :], AX))
        S.op('scalar', lambda e, b=b: e.activation(b.mu[:, 0:1], b.mu[:, 2:3], AF.Sqrt, bias=1e-5, scale=1.0 / 4096))
        S.op('vector', lambda e, b=b: e.reciprocal(b.mu[:, 3:4], b.mu[:, 0:1]))
        S.op('vector', lambda e, b=b: e.tensor_single_scalar(b.lnb[:, :], b.xc[:, :], b.mu[:, 3:4], ALU.mult))
        for g in range(NKD // 4):
            for i in range(4):
                dt = g * 4 + i
                S.op('tensor', lambda e, b=b, dt=dt, i=i: e.transpose(
                    b.PT[:, i * 128:(i + 1) * 128], b.lnb[:, dt * 128:(dt + 1) * 128], b.idb[:, :]),
                    wait=(i == 0))
            for i in range(4):
                dt = g * 4 + i
                S.op('scalar', lambda e, b=b, dt=dt, t=t, i=i: e.copy(
                    b.lnT[:, dt * 1024 + t * 128: dt * 1024 + t * 128 + 128], b.PT[:, i * 128:(i + 1) * 128]),
                    wait=(i == 0))
        if per_t is not None:
            per_t(t)


def emit_gemm(S, wt_ap, n_oc, b, lhsT_fn, pre_oc=None, post=None, dequant=False, n_tt=NTT, n_kd=NKD):
    for oc in range(n_oc):
        for q in range(n_kd // 4):
            for j in range(4):
                dt = q * 4 + j
                S.dma(b.stage[:, j * 512:(j + 1) * 512],
                      wt_ap[dt * 128:(dt + 1) * 128, oc * 512:(oc + 1) * 512], wait=(j == 0))
            if not dequant:
                S.op('vector', lambda e, b=b, q=q: e.tensor_copy(b.wtb[:, q * 2048:(q + 1) * 2048], b.stage[:, :]))
            else:
                first = True
                for j in range(4):
                    dt = q * 4 + j
                    for ob in range(4):
                        og = oc * 4 + ob
                        S.op('vector', lambda e, b=b, dt=dt, j=j, ob=ob, og=og: e.tensor_single_scalar(
                            b.wtb[:, dt * 512 + ob * 128: dt * 512 + ob * 128 + 128],
                            b.stage[:, j * 512 + ob * 128: j * 512 + ob * 128 + 128],
                            b.sc[:, dt * 32 + og: dt * 32 + og + 1], ALU.mult), wait=first)
                        first = False
        if pre_oc is not None:
            pre_oc(oc)
        for tt in range(n_tt):
            PG = b.PGs[tt % 2]
            mk = b.pg_marks.get(tt % 2)
            for dt in range(n_kd):
                ap = lhsT_fn(tt, dt)
                S.op('tensor', lambda e, PG=PG, ap=ap, dt=dt, b=b: e.matmul(
                    PG[:, :], ap, b.wtb[:, dt * 512:(dt + 1) * 512],
                    start=(dt == 0), stop=(dt == n_kd - 1)),
                    wait=(dt == 0 and tt == 0),
                    extra=([mk] if (dt == 0 and mk is not None) else ()))
            post(tt, oc, PG)
            b.pg_marks[tt % 2] = S.last


def load_conv(S, b, hbm_ap, dst, total_cols, tile_cols):
    per = 2048 // tile_cols
    ntiles = total_cols // tile_cols
    done = 0
    while done < ntiles:
        n = min(per, ntiles - done)
        for j in range(n):
            dt = done + j
            S.dma(b.stage[:, j * tile_cols:(j + 1) * tile_cols],
                  hbm_ap[dt * 128:(dt + 1) * 128, :], wait=(j == 0))
        S.op('vector', lambda e, b=b, done=done, n=n, tc=tile_cols: e.tensor_copy(
            dst[:, done * tc: (done + n) * tc], b.stage[:, :n * tc]))
        done += n


def build_p1():
    nc = bass.Bass("TRN2", target_bir_lowering=False)
    xn = nc.dram_tensor("xn", [TOK, D], F32, kind="ExternalInput").ap()
    wT = nc.dram_tensor("wT", [D, D], F32, kind="ExternalInput").ap()
    gw1T = nc.dram_tensor("gw1T", [D, 32], F32, kind="ExternalInput").ap()
    gw2T = nc.dram_tensor("gw2T", [32, 8], F32, kind="ExternalInput").ap()
    gb1b = nc.dram_tensor("gb1b", [128, 32], F32, kind="ExternalInput").ap()
    gamb = nc.dram_tensor("gamb", [128, 32], F32, kind="ExternalInput").ap()
    betb = nc.dram_tensor("betb", [128, 32], F32, kind="ExternalInput").ap()
    gb2b = nc.dram_tensor("gb2b", [128, 8], F32, kind="ExternalInput").ap()
    lAf = nc.dram_tensor("lAf", [D, 128], F32, kind="ExternalInput").ap()
    idb_h = nc.dram_tensor("idb", [128, 128], BF16, kind="ExternalInput").ap()
    onesf_h = nc.dram_tensor("onesf", [128, 1], F32, kind="ExternalInput").ap()
    main = nc.dram_tensor("main", [TOK, D], F32, kind="ExternalOutput").ap()
    tmp = nc.dram_tensor("tmp", [TOK, 128], F32, kind="ExternalOutput").ap()
    wg = nc.dram_tensor("wg", [TOK, 16], F32, kind="ExternalOutput").ap()
    pc = nc.dram_tensor("pc", [1, 16], F32, kind="ExternalOutput").ap()

    es = contextlib.ExitStack()
    with es:
        b = B()
        sb = _alloc_common(nc, es, b)
        ps = lambda name, shape, dt=F32: es.enter_context(nc.psum_tensor(name, shape, dt))
        b.PGs = [ps("PGa", [128, 512]), ps("PGb", [128, 512])]; b.pg_marks = {}
        b.PT = ps("PT", [128, 512], BF16); b.PH = ps("PH", [128, 32])
        b.PT32 = ps("PT32", [128, 128], BF16); b.P8 = ps("P8", [128, 8]); b.PC = ps("PC", [128, 16])
        b.PTMP = ps("PTMP", [128, 128])
        gw1b = sb("gw1b", [128, NKD * 32], BF16)
        gw2b = sb("gw2b", [128, 8], BF16)
        lab = sb("lab", [128, NKD * 128], BF16)
        gb1s = sb("gb1s", [128, 32], F32); gams = sb("gams", [128, 32], F32)
        bets = sb("bets", [128, 32], F32); gb2s = sb("gb2s", [128, 8], F32)
        onesf = sb("onesfs", [128, 1], F32)
        oh = sb("oh", [128, 128], F32)
        h32 = sb("h32", [128, 32], F32); h32c = sb("h32c", [128, 32], F32)
        h32b = sb("h32b", [128, 32], BF16); h32T = sb("h32T", [128, 128], BF16)
        g8 = sb("g8", [128, 8], F32); m4 = sb("m4", [128, 4], F32)
        wgs = sb("wgs", [128, 16], F32); tev = sb("tev", [128, 128], F32)
        ev = sb("ev", [128, 512], F32)

        S = Seq()
        emit_consts(nc, es, S, LN_CONSTS)
        S.dma(b.idb[:, :], idb_h[:, :])
        S.dma(onesf[:, :], onesf_h[:, :], wait=False)
        S.dma(gb1s[:, :], gb1b[:, :], wait=False)
        S.dma(gams[:, :], gamb[:, :], wait=False)
        S.dma(bets[:, :], betb[:, :], wait=False)
        S.dma(gb2s[:, :], gb2b[:, :], wait=False)
        for dt in range(NKD):
            S.dma(b.stage[:, dt * 32:(dt + 1) * 32], gw1T[dt * 128:(dt + 1) * 128, :], wait=(dt == 0))
        S.op('vector', lambda e: e.tensor_copy(gw1b[:, :], b.stage[:, :NKD * 32]))
        S.dma(b.stage[0:32, 0:8], gw2T[0:32, 0:8])
        S.op('vector', lambda e: e.tensor_copy(gw2b[0:32, :], b.stage[0:32, 0:8]))
        load_conv(S, b, lAf, lab, NKD * 128, 128)

        def gating(t):
            for dt in range(NKD):
                S.op('tensor', lambda e, dt=dt, t=t: e.matmul(
                    b.PH[:, :], b.lnT[:, dt * 1024 + t * 128: dt * 1024 + t * 128 + 128],
                    gw1b[:, dt * 32:(dt + 1) * 32], start=(dt == 0), stop=(dt == NKD - 1)), wait=(dt == 0))
            S.op('vector', lambda e: e.tensor_add(h32[:, :], b.PH[:, :], gb1s[:, :]))
            S.op('vector', lambda e: e.reduce_sum(m4[:, 0:1], h32[:, :], AX))
            S.op('scalar', lambda e: e.mul(m4[:, 1:2], m4[:, 0:1], 1.0 / 32))
            S.op('vector', lambda e: e.tensor_single_scalar(h32c[:, :], h32[:, :], m4[:, 1:2], ALU.subtract))
            S.op('scalar', lambda e: e.activation(h32[:, :], h32c[:, :], AF.Square))
            S.op('vector', lambda e: e.reduce_sum(m4[:, 2:3], h32[:, :], AX))
            S.op('scalar', lambda e: e.activation(m4[:, 0:1], m4[:, 2:3], AF.Sqrt, bias=1e-5, scale=1.0 / 32))
            S.op('vector', lambda e: e.reciprocal(m4[:, 3:4], m4[:, 0:1]))
            S.op('vector', lambda e: e.tensor_single_scalar(h32[:, :], h32c[:, :], m4[:, 3:4], ALU.mult))
            S.op('vector', lambda e: e.tensor_mul(h32c[:, :], h32[:, :], gams[:, :]))
            S.op('vector', lambda e: e.tensor_add(h32[:, :], h32c[:, :], bets[:, :]))
            S.op('scalar', lambda e: e.activation(h32c[:, :], h32[:, :], AF.Relu))
            S.op('vector', lambda e: e.tensor_copy(h32b[:, :], h32c[:, :]))
            S.op('tensor', lambda e: e.transpose(b.PT32[0:32, 0:128], h32b[:, :], b.idb[:, :]))
            S.op('scalar', lambda e: e.copy(h32T[0:32, :], b.PT32[0:32, 0:128]))
            S.op('tensor', lambda e: e.matmul(b.P8[:, :], h32T[0:32, 0:128], gw2b[0:32, :], start=True, stop=True))
            S.op('vector', lambda e: e.tensor_add(g8[:, :], b.P8[:, :], gb2s[:, :]))
            S.op('vector', lambda e: e.reduce_max(m4[:, 0:1], g8[:, :], AX))
            S.op('vector', lambda e, t=t: e.tensor_single_scalar(oh[:, t * 16:t * 16 + 8], g8[:, :], m4[:, 0:1], ALU.is_equal))
            S.op('vector', lambda e, t=t: e.scalar_tensor_tensor(
                wgs[:, 0:8], oh[:, t * 16:t * 16 + 8], -1e9, g8[:, :], ALU.mult, ALU.add))
            S.op('vector', lambda e: e.reduce_max(m4[:, 1:2], wgs[:, 0:8], AX))
            S.op('vector', lambda e, t=t: e.tensor_single_scalar(oh[:, t * 16 + 8:t * 16 + 16], wgs[:, 0:8], m4[:, 1:2], ALU.is_equal))
            S.op('vector', lambda e: e.tensor_sub(m4[:, 2:3], m4[:, 0:1], m4[:, 1:2]))
            S.op('scalar', lambda e: e.activation(m4[:, 3:4], m4[:, 2:3], AF.Sigmoid))
            S.op('vector', lambda e, t=t: e.tensor_single_scalar(wgs[:, 0:8], oh[:, t * 16:t * 16 + 8], m4[:, 3:4], ALU.mult))
            S.op('vector', lambda e: e.tensor_scalar(m4[:, 2:3], m4[:, 3:4], -1.0, 1.0, ALU.mult, ALU.add))
            S.op('vector', lambda e, t=t: e.tensor_single_scalar(wgs[:, 8:16], oh[:, t * 16 + 8:t * 16 + 16], m4[:, 2:3], ALU.mult))
            S.dma(wg[t * 128:(t + 1) * 128, :], wgs[:, :])
            for dt in range(NKD):
                S.op('tensor', lambda e, dt=dt, t=t: e.matmul(
                    b.PTMP[:, :], b.lnT[:, dt * 1024 + t * 128: dt * 1024 + t * 128 + 128],
                    lab[:, dt * 128:(dt + 1) * 128], start=(dt == 0), stop=(dt == NKD - 1)), wait=(dt == 0))
            S.op('scalar', lambda e: e.copy(tev[:, :], b.PTMP[:, :]))
            S.dma(tmp[t * 128:(t + 1) * 128, :], tev[:, :])

        emit_ln_phase(S, xn, b, per_t=gating)

        for t in range(NTT):
            S.op('tensor', lambda e, t=t: e.matmul(b.PC[0:1, 0:8], onesf[:, 0:1], oh[:, t * 16:t * 16 + 8],
                                                   start=(t == 0), stop=(t == NTT - 1)), wait=(t == 0))
        for t in range(NTT):
            S.op('tensor', lambda e, t=t: e.matmul(b.PC[0:1, 8:16], onesf[:, 0:1], oh[:, t * 16 + 8:t * 16 + 16],
                                                   start=(t == 0), stop=(t == NTT - 1)), wait=(t == 0))
        S.op('vector', lambda e: e.tensor_copy(wgs[0:1, 0:16], b.PC[0:1, 0:16]))
        S.dma(pc[0:1, :], wgs[0:1, 0:16])

        def post(tt, oc, PG):
            S.op('scalar', lambda e, PG=PG: e.copy(ev[:, :], PG[:, :]))
            S.dma(main[tt * 128:(tt + 1) * 128, oc * 512:(oc + 1) * 512], ev[:, :])

        emit_gemm(S, wT, 8, b,
                  lambda tt, dt: b.lnT[:, dt * 1024 + tt * 128: dt * 1024 + tt * 128 + 128],
                  post=post)
        S.fin()
        _emit(nc, S, es)
    return nc


def build_p2():
    nc = bass.Bass("TRN2", target_bir_lowering=False)
    xn = nc.dram_tensor("xn", [TOK, D], F32, kind="ExternalInput").ap()
    main = nc.dram_tensor("main", [TOK, D], F32, kind="ExternalInput").ap()
    tmp = nc.dram_tensor("tmp", [TOK, 128], F32, kind="ExternalInput").ap()
    wg = nc.dram_tensor("wg", [TOK, 16], F32, kind="ExternalInput").ap()
    pcall = nc.dram_tensor("pcall", [8, 16], F32, kind="ExternalInput").ap()
    lBf = nc.dram_tensor("lBf", [128, D], F32, kind="ExternalInput").ap()
    idb_h = nc.dram_tensor("idb", [128, 128], BF16, kind="ExternalInput").ap()
    onesf_h = nc.dram_tensor("onesf", [128, 1], F32, kind="ExternalInput").ap()
    onesr_h = nc.dram_tensor("onesr", [1, 128], F32, kind="ExternalInput").ap()
    out = nc.dram_tensor("out", [TOK, D], F32, kind="ExternalOutput").ap()

    es = contextlib.ExitStack()
    with es:
        sb = lambda name, shape, dt: es.enter_context(nc.sbuf_tensor(name, shape, dt))
        ps = lambda name, shape, dt=F32: es.enter_context(nc.psum_tensor(name, shape, dt))
        b = B()
        b.idb = sb("idb_sb", [128, 128], BF16)
        PG = ps("PG", [128, 512]); PT = ps("PT", [128, 128], BF16); PC2 = ps("PC2", [128, 16]); PB = ps("PB", [128, 16])
        stage = sb("stage", [128, 2048], F32)
        pcs = sb("pcs", [128, 16], F32); cnt = sb("cnt", [128, 16], F32); alw = sb("alw", [128, 16], F32)
        onesf = sb("onesfs", [128, 1], F32); onesr = sb("onesrs", [128, 128], F32)
        lbb = sb("lbb", [128, D], BF16)
        x_sb = sb("x_sb", [128, D], F32); mn = sb("mn", [128, D], F32)
        wgt = sb("wgt", [128, 16], F32); wgt2 = sb("wgt2", [128, 16], F32); w8 = sb("w8", [128, 8], F32)
        tpt = sb("tpt", [128, 128], F32); tps = sb("tps", [128, 128], F32); tpsb = sb("tpsb", [128, 128], BF16)
        tpstb = sb("tpstb", [128, 128], BF16)
        ev = sb("ev", [128, 512], F32); osb = sb("osb", [128, D], F32)

        S = Seq()
        emit_consts(nc, es, S, LN_CONSTS)
        S.dma(b.idb[:, :], idb_h[:, :])
        S.dma(onesf[:, :], onesf_h[:, :], wait=False)
        S.dma(onesr[0:1, 0:128], onesr_h[0:1, :], wait=False)
        S.dma(pcs[0:8, 0:16], pcall[:, :], wait=False)
        S.op('tensor', lambda e: e.matmul(PC2[0:1, 0:16], onesf[0:8, 0:1], pcs[0:8, 0:16], start=True, stop=True))
        S.op('scalar', lambda e: e.copy(cnt[0:1, 0:16], PC2[0:1, 0:16]))
        S.op('tensor', lambda e: e.matmul(PB[:, :], onesr[0:1, 0:128], cnt[0:1, 0:16], start=True, stop=True))
        S.op('vector', lambda e: e.tensor_single_scalar(alw[:, :], PB[:, :], 3072.5, ALU.is_le))
        b.stage = stage
        for h in range(2):
            S.dma(stage[:, :], lBf[:, h * 2048:(h + 1) * 2048])
            S.op('vector', lambda e, h=h: e.tensor_single_scalar(
                lbb[:, h * 2048:(h + 1) * 2048], stage[:, :], 1.0 / 16, ALU.mult))

        for t in range(NTT):
            S.dma(wgt[:, :], wg[t * 128:(t + 1) * 128, :])
            S.dma(tpt[:, :], tmp[t * 128:(t + 1) * 128, :], wait=False)
            S.dma(x_sb[:, :], xn[t * 128:(t + 1) * 128, :], wait=False)
            S.dma(mn[:, :], main[t * 128:(t + 1) * 128, :], wait=False)
            S.op('vector', lambda e: e.tensor_mul(wgt2[:, :], wgt[:, :], alw[:, :]))
            S.op('vector', lambda e: e.tensor_add(w8[:, :], wgt2[:, 0:8], wgt2[:, 8:16]))
            for ex in range(8):
                S.op('vector', lambda e, ex=ex: e.tensor_single_scalar(
                    tps[:, ex * 16:(ex + 1) * 16], tpt[:, ex * 16:(ex + 1) * 16], w8[:, ex:ex + 1], ALU.mult))
            S.op('vector', lambda e: e.tensor_copy(tpsb[:, :], tps[:, :]))
            S.op('tensor', lambda e: e.transpose(PT[:, :], tpsb[:, :], b.idb[:, :]))
            S.op('scalar', lambda e: e.copy(tpstb[:, :], PT[:, :]))
            for oc in range(8):
                S.op('tensor', lambda e, oc=oc: e.matmul(PG[:, :], tpstb[:, :], lbb[:, oc * 512:(oc + 1) * 512],
                                                         start=True, stop=True))
                S.op('vector', lambda e, oc=oc: e.tensor_add(ev[:, :], PG[:, :], mn[:, oc * 512:(oc + 1) * 512]))
                S.op('vector', lambda e, oc=oc: e.scalar_tensor_tensor(
                    osb[:, oc * 512:(oc + 1) * 512], ev[:, :], 0.5, x_sb[:, oc * 512:(oc + 1) * 512], ALU.mult, ALU.add))
            S.dma(out[t * 128:(t + 1) * 128, :], osb[:, :])
        S.fin()
        _emit(nc, S, es)
    return nc


def build_p3a():
    nc = bass.Bass("TRN2", target_bir_lowering=False)
    xn = nc.dram_tensor("xn", [TOK, D], F32, kind="ExternalInput").ap()
    wqT = nc.dram_tensor("wqT", [D, D], F32, kind="ExternalInput").ap()
    wkT = nc.dram_tensor("wkT", [D, 1024], F32, kind="ExternalInput").ap()
    wvT = nc.dram_tensor("wvT", [D, 1024], F32, kind="ExternalInput").ap()
    bqb = nc.dram_tensor("bqb", [128, D], F32, kind="ExternalInput").ap()
    bkb = nc.dram_tensor("bkb", [128, 1024], F32, kind="ExternalInput").ap()
    bvb = nc.dram_tensor("bvb", [128, 1024], F32, kind="ExternalInput").ap()
    idb_h = nc.dram_tensor("idb", [128, 128], BF16, kind="ExternalInput").ap()
    onesb_h = nc.dram_tensor("onesb", [128, 1], BF16, kind="ExternalInput").ap()
    q_o = nc.dram_tensor("q", [TOK, D], F32, kind="ExternalOutput").ap()
    Mp = nc.dram_tensor("Mp", [1024, 128], F32, kind="ExternalOutput").ap()
    ksp = nc.dram_tensor("ksp", [1, 1024], F32, kind="ExternalOutput").ap()

    es = contextlib.ExitStack()
    with es:
        b = B()
        sb = _alloc_common(nc, es, b)
        ps = lambda name, shape, dt=F32: es.enter_context(nc.psum_tensor(name, shape, dt))
        b.PGs = [ps("PGa", [128, 512]), ps("PGb", [128, 512])]; b.pg_marks = {}
        b.PT = ps("PT", [128, 512], BF16)
        PM = ps("PM", [128, 128]); PR = ps("PR", [128, 512])
        k_sb = sb("k_sb", [128, NTT * 1024], BF16)
        v_sb = sb("v_sb", [128, NTT * 1024], BF16)
        onesb = sb("onesbs", [128, 1], BF16)
        bias_c = sb("bias_c", [128, 512], F32)
        qs = sb("qs", [128, 512], F32); rb = sb("rb", [128, 512], F32)
        eb = sb("eb", [128, 512], F32)

        S = Seq()
        emit_consts(nc, es, S, LN_CONSTS)
        S.dma(b.idb[:, :], idb_h[:, :])
        S.dma(onesb[:, :], onesb_h[:, :], wait=False)
        emit_ln_phase(S, xn, b)

        def phi_post(bias_hbm, out_hbm, store_sb):
            def pre_oc(oc):
                S.dma(bias_c[:, :], bias_hbm[:, oc * 512:(oc + 1) * 512])

            def post(tt, oc, PG):
                S.op('vector', lambda e, PG=PG: e.tensor_add(qs[:, :], PG[:, :], bias_c[:, :]))
                S.op('scalar', lambda e: e.activation(rb[:, :], qs[:, :], AF.Relu))
                S.op('vector', lambda e: e.tensor_sub(eb[:, :], qs[:, :], rb[:, :]))
                S.op('scalar', lambda e: e.activation(qs[:, :], eb[:, :], AF.Exp))
                S.op('vector', lambda e: e.tensor_add(eb[:, :], rb[:, :], qs[:, :]))
                if out_hbm is not None:
                    S.dma(out_hbm[tt * 128:(tt + 1) * 128, oc * 512:(oc + 1) * 512], eb[:, :])
                if store_sb is not None:
                    S.op('vector', lambda e, tt=tt, oc=oc: e.tensor_copy(
                        store_sb[:, tt * 1024 + oc * 512: tt * 1024 + oc * 512 + 512], eb[:, :]))
            return pre_oc, post

        pq, po = phi_post(bqb, q_o, None)
        emit_gemm(S, wqT, 8, b, lambda tt, dt: b.lnT[:, dt * 1024 + tt * 128: dt * 1024 + tt * 128 + 128],
                  pre_oc=pq, post=po)
        pq, po = phi_post(bkb, None, k_sb)
        emit_gemm(S, wkT, 2, b, lambda tt, dt: b.lnT[:, dt * 1024 + tt * 128: dt * 1024 + tt * 128 + 128],
                  pre_oc=pq, post=po)

        def pre_v(oc):
            S.dma(bias_c[:, :], bvb[:, oc * 512:(oc + 1) * 512])

        def post_v(tt, oc, PG):
            S.op('vector', lambda e, PG=PG: e.tensor_add(qs[:, :], PG[:, :], bias_c[:, :]))
            S.op('vector', lambda e, tt=tt, oc=oc: e.tensor_copy(
                v_sb[:, tt * 1024 + oc * 512: tt * 1024 + oc * 512 + 512], qs[:, :]))
        emit_gemm(S, wvT, 2, b, lambda tt, dt: b.lnT[:, dt * 1024 + tt * 128: dt * 1024 + tt * 128 + 128],
                  pre_oc=pre_v, post=post_v)

        for kvh in range(8):
            for tt in range(NTT):
                S.op('tensor', lambda e, kvh=kvh, tt=tt: e.matmul(
                    PM[:, :], k_sb[:, tt * 1024 + kvh * 128: tt * 1024 + kvh * 128 + 128],
                    v_sb[:, tt * 1024 + kvh * 128: tt * 1024 + kvh * 128 + 128],
                    start=(tt == 0), stop=(tt == NTT - 1)), wait=(tt == 0))
            S.op('scalar', lambda e: e.copy(rb[:, 0:128], PM[:, :]))
            S.dma(Mp[kvh * 128:(kvh + 1) * 128, :], rb[:, 0:128])
        for half in range(2):
            for tt in range(NTT):
                S.op('tensor', lambda e, half=half, tt=tt: e.matmul(
                    PR[0:1, 0:512], onesb[:, 0:1], k_sb[:, tt * 1024 + half * 512: tt * 1024 + half * 512 + 512],
                    start=(tt == 0), stop=(tt == NTT - 1)), wait=(tt == 0))
            S.op('scalar', lambda e: e.copy(qs[0:1, 0:512], PR[0:1, 0:512]))
            S.dma(ksp[0:1, half * 512:(half + 1) * 512], qs[0:1, 0:512])
        S.fin()
        _emit(nc, S, es)
    return nc


def build_p3b():
    nc = bass.Bass("TRN2", target_bir_lowering=False)
    qT = nc.dram_tensor("qT", [D, TOK], F32, kind="ExternalInput").ap()
    xn = nc.dram_tensor("xn", [TOK, D], F32, kind="ExternalInput").ap()
    Mall = nc.dram_tensor("Mall", [4096, 128], F32, kind="ExternalInput").ap()
    ksall = nc.dram_tensor("ksall", [4, 1024], F32, kind="ExternalInput").ap()
    woT = nc.dram_tensor("woT", [D, D], F32, kind="ExternalInput").ap()
    bob = nc.dram_tensor("bob", [128, D], F32, kind="ExternalInput").ap()
    idb_h = nc.dram_tensor("idb", [128, 128], BF16, kind="ExternalInput").ap()
    idf_h = nc.dram_tensor("idf", [128, 128], F32, kind="ExternalInput").ap()
    onesf_h = nc.dram_tensor("onesf", [128, 1], F32, kind="ExternalInput").ap()
    out = nc.dram_tensor("out", [TOK, D], F32, kind="ExternalOutput").ap()

    es = contextlib.ExitStack()
    with es:
        sb = lambda name, shape, dt: es.enter_context(nc.sbuf_tensor(name, shape, dt))
        ps = lambda name, shape, dt=F32: es.enter_context(nc.psum_tensor(name, shape, dt))
        b = B()
        b.idb = sb("idb_sb", [128, 128], BF16)
        idf = sb("idf_sb", [128, 128], F32)
        b.stage = sb("stage", [128, 2048], F32)
        b.wtb = sb("wtb", [128, NKD * 512], BF16)
        b.PGs = [ps("PGa", [128, 512]), ps("PGb", [128, 512])]; b.pg_marks = {}
        PA = ps("PA", [128, 512]); PR = ps("PR", [128, 512]); PTq = ps("PTq", [128, 128])
        attb = sb("attb", [128, 32 * 1024], BF16)
        Mgb = sb("Mgb", [128, 1024], BF16)
        mtmp = sb("mtmp", [128, 512], F32); ma = sb("ma", [128, 128], F32)
        mb_ = sb("mb_", [128, 128], F32); mc = sb("mc", [128, 128], F32)
        ks4 = sb("ks4", [128, 1024], F32); ksr = sb("ksr", [128, 1024], F32); ksr2 = sb("ksr2", [128, 1024], F32)
        rk = sb("rk", [128, 8], F32); ev1 = sb("ev1", [128, 1], F32)
        onesf = sb("onesfs", [128, 1], F32)
        qtb = sb("qtb", [128, 1024], BF16); qstage = sb("qstage", [128, 1024], F32)
        bias_c = sb("bias_c", [128, 512], F32); xq = sb("xq", [128, 512], F32)
        ev = sb("ev", [128, 512], F32); ev2 = sb("ev2", [128, 512], F32)

        S = Seq()
        emit_consts(nc, es, S, LN_CONSTS)
        S.dma(b.idb[:, :], idb_h[:, :])
        S.dma(idf[:, :], idf_h[:, :], wait=False)
        S.dma(onesf[:, :], onesf_h[:, :], wait=False)
        for kvh in range(8):
            for j in range(4):
                S.dma(mtmp[:, j * 128:(j + 1) * 128], Mall[j * 1024 + kvh * 128: j * 1024 + kvh * 128 + 128, :],
                      wait=(j == 0))
            S.op('vector', lambda e: e.tensor_add(ma[:, :], mtmp[:, 0:128], mtmp[:, 128:256]))
            S.op('vector', lambda e: e.tensor_add(mb_[:, :], mtmp[:, 256:384], mtmp[:, 384:512]))
            S.op('vector', lambda e: e.tensor_add(mc[:, :], ma[:, :], mb_[:, :]))
            S.op('vector', lambda e, kvh=kvh: e.tensor_copy(Mgb[:, kvh * 128:(kvh + 1) * 128], mc[:, :]))
        S.dma(ks4[0:4, 0:1024], ksall[:, :])
        for half in range(2):
            S.op('tensor', lambda e, half=half: e.matmul(
                PR[0:1, 0:512], onesf[0:4, 0:1], ks4[0:4, half * 512:(half + 1) * 512], start=True, stop=True))
            S.op('scalar', lambda e, half=half: e.copy(ksr[0:1, half * 512:(half + 1) * 512], PR[0:1, 0:512]))
        S.op('scalar', lambda e: e.add(ksr2[0:1, 0:1024], ksr[0:1, 0:1024], 1e-10))
        for kvh in range(8):
            S.op('tensor', lambda e, kvh=kvh: e.transpose(PTq[:, 0:1], ksr2[0:1, kvh * 128:(kvh + 1) * 128], idf[0:1, 0:1]))
            S.op('scalar', lambda e: e.copy(ev1[:, 0:1], PTq[:, 0:1]))
            S.op('vector', lambda e, kvh=kvh: e.reciprocal(rk[:, kvh:kvh + 1], ev1[:, 0:1]))
        for qh in range(32):
            kvh = qh // 4
            S.dma(qstage[:, :], qT[qh * 128:(qh + 1) * 128, :])
            S.op('vector', lambda e: e.tensor_copy(qtb[:, :], qstage[:, :]))
            for half in range(2):
                S.op('tensor', lambda e, kvh=kvh, half=half: e.matmul(
                    PA[:, :], Mgb[:, kvh * 128:(kvh + 1) * 128], qtb[:, half * 512:(half + 1) * 512],
                    start=True, stop=True))
                S.op('vector', lambda e, qh=qh, kvh=kvh, half=half: e.tensor_single_scalar(
                    attb[:, qh * 1024 + half * 512: qh * 1024 + half * 512 + 512], PA[:, :],
                    rk[:, kvh:kvh + 1], ALU.mult))

        def pre_oc(oc):
            S.dma(bias_c[:, :], bob[:, oc * 512:(oc + 1) * 512])

        def post(tt, oc, PG):
            S.dma(xq[:, :], xn[tt * 128:(tt + 1) * 128, oc * 512:(oc + 1) * 512])
            S.op('vector', lambda e, PG=PG: e.tensor_add(ev[:, :], PG[:, :], bias_c[:, :]))
            S.op('vector', lambda e: e.tensor_add(ev2[:, :], ev[:, :], xq[:, :]))
            S.dma(out[tt * 128:(tt + 1) * 128, oc * 512:(oc + 1) * 512], ev2[:, :])

        emit_gemm(S, woT, 8, b, lambda tt, dt: attb[:, dt * 1024 + tt * 128: dt * 1024 + tt * 128 + 128],
                  pre_oc=pre_oc, post=post)
        S.fin()
        _emit(nc, S, es)
    return nc


def build_p4a():
    nc = bass.Bass("TRN2", target_bir_lowering=False)
    xn = nc.dram_tensor("xn", [TOK, D], F32, kind="ExternalInput").ap()
    wqT = nc.dram_tensor("wqT", [D, D], F32, kind="ExternalInput").ap()
    wkT = nc.dram_tensor("wkT", [D, D], F32, kind="ExternalInput").ap()
    wvT = nc.dram_tensor("wvT", [D, D], F32, kind="ExternalInput").ap()
    cosc = nc.dram_tensor("cosc", [TOK, 128], F32, kind="ExternalInput").ap()
    sinc = nc.dram_tensor("sinc", [TOK, 128], F32, kind="ExternalInput").ap()
    idb_h = nc.dram_tensor("idb", [128, 128], BF16, kind="ExternalInput").ap()
    q_o = nc.dram_tensor("q", [TOK, D], F32, kind="ExternalOutput").ap()
    k_o = nc.dram_tensor("k", [TOK, D], F32, kind="ExternalOutput").ap()
    v_o = nc.dram_tensor("v", [TOK, D], F32, kind="ExternalOutput").ap()

    es = contextlib.ExitStack()
    with es:
        b = B()
        sb = _alloc_common(nc, es, b)
        ps = lambda name, shape, dt=F32: es.enter_context(nc.psum_tensor(name, shape, dt))
        b.PGs = [ps("PGa", [128, 512]), ps("PGb", [128, 512])]; b.pg_marks = {}
        b.PT = ps("PT", [128, 512], BF16)
        cos_sb = sb("cos_sb", [128, 1024], F32); sin_sb = sb("sin_sb", [128, 1024], F32)
        qs = sb("qs", [128, 512], F32); ro = sb("ro", [128, 512], F32)
        u1 = sb("u1", [128, 64], F32); u2 = sb("u2", [128, 64], F32)

        S = Seq()
        emit_consts(nc, es, S, LN_CONSTS)
        S.dma(b.idb[:, :], idb_h[:, :])
        for tt in range(NTT):
            S.dma(cos_sb[:, tt * 128:(tt + 1) * 128], cosc[tt * 128:(tt + 1) * 128, :], wait=(tt == 0))
        for tt in range(NTT):
            S.dma(sin_sb[:, tt * 128:(tt + 1) * 128], sinc[tt * 128:(tt + 1) * 128, :], wait=False)
        emit_ln_phase(S, xn, b)

        def rope_post(out_hbm):
            def post(tt, oc, PG):
                S.op('scalar', lambda e, PG=PG: e.copy(qs[:, :], PG[:, :]))
                for h4 in range(4):
                    base = h4 * 128
                    c1 = cos_sb[:, tt * 128: tt * 128 + 64]
                    c2 = cos_sb[:, tt * 128 + 64: tt * 128 + 128]
                    s1 = sin_sb[:, tt * 128: tt * 128 + 64]
                    s2 = sin_sb[:, tt * 128 + 64: tt * 128 + 128]
                    S.op('vector', lambda e, base=base, s1=s1: e.tensor_mul(u1[:, :], qs[:, base + 64:base + 128], s1))
                    S.op('vector', lambda e, base=base, c1=c1: e.tensor_mul(u2[:, :], qs[:, base:base + 64], c1))
                    S.op('vector', lambda e, base=base: e.tensor_sub(ro[:, base:base + 64], u2[:, :], u1[:, :]))
                    S.op('vector', lambda e, base=base, s2=s2: e.tensor_mul(u1[:, :], qs[:, base:base + 64], s2))
                    S.op('vector', lambda e, base=base, c2=c2: e.tensor_mul(u2[:, :], qs[:, base + 64:base + 128], c2))
                    S.op('vector', lambda e, base=base: e.tensor_add(ro[:, base + 64:base + 128], u2[:, :], u1[:, :]))
                S.dma(out_hbm[tt * 128:(tt + 1) * 128, oc * 512:(oc + 1) * 512], ro[:, :])
            return post

        lhsT = lambda tt, dt: b.lnT[:, dt * 1024 + tt * 128: dt * 1024 + tt * 128 + 128]
        emit_gemm(S, wqT, 8, b, lhsT, post=rope_post(q_o))
        emit_gemm(S, wkT, 8, b, lhsT, post=rope_post(k_o))

        def post_v(tt, oc, PG):
            S.op('scalar', lambda e, PG=PG: e.copy(qs[:, :], PG[:, :]))
            S.dma(v_o[tt * 128:(tt + 1) * 128, oc * 512:(oc + 1) * 512], qs[:, :])
        emit_gemm(S, wvT, 8, b, lhsT, post=post_v)
        S.fin()
        _emit(nc, S, es)
    return nc


def build_p4b():
    nc = bass.Bass("TRN2", target_bir_lowering=False)
    qTp = nc.dram_tensor("qTp", [1024, 4096], F32, kind="ExternalInput").ap()
    kTp = nc.dram_tensor("kTp", [1024, 4096], F32, kind="ExternalInput").ap()
    vp = nc.dram_tensor("vp", [8 * 4096, 128], F32, kind="ExternalInput").ap()
    idf_h = nc.dram_tensor("idf", [128, 128], F32, kind="ExternalInput").ap()
    onesb_h = nc.dram_tensor("onesb", [128, 1], BF16, kind="ExternalInput").ap()
    Op = nc.dram_tensor("Op", [8 * 4096, 128], F32, kind="ExternalOutput").ap()

    es = contextlib.ExitStack()
    with es:
        sb = lambda name, shape, dt: es.enter_context(nc.sbuf_tensor(name, shape, dt))
        ps = lambda name, shape, dt=F32: es.enter_context(nc.psum_tensor(name, shape, dt))
        PSa = ps("PSa", [128, 512]); PSb = ps("PSb", [128, 512])
        POa = ps("POa", [128, 512]); POb = ps("POb", [128, 512])
        PD0 = ps("PD0", [128, 512]); PD1 = ps("PD1", [128, 512])
        PTq = ps("PTq", [128, 128])
        idf = sb("idf_sb", [128, 128], F32)
        onesb = sb("onesbs", [128, 1], BF16)
        stage = sb("stage", [128, 2048], F32)
        ktb = sb("ktb", [128, 4096], BF16); qtb = sb("qtb", [128, 4096], BF16)
        vsb = sb("vsb", [128, 4096], BF16)
        expb = sb("expb", [128, 1024], BF16)
        dn = sb("dn", [128, 1024], F32)
        d1 = sb("d1", [128, 1], F32); d2 = sb("d2", [128, 1], F32)
        oev = sb("oev", [128, 128], F32)
        SCALE = 1.0 / float(np.sqrt(128.0))

        S = Seq()
        emit_consts(nc, es, S, LN_CONSTS)
        S.dma(idf[:, :], idf_h[:, :])
        S.dma(onesb[:, :], onesb_h[:, :], wait=False)
        for p in range(8):
            for h in range(2):
                S.dma(stage[:, :], kTp[p * 128:(p + 1) * 128, h * 2048:(h + 1) * 2048])
                S.op('vector', lambda e, h=h: e.tensor_copy(ktb[:, h * 2048:(h + 1) * 2048], stage[:, :]))
            for h in range(2):
                S.dma(stage[:, :], qTp[p * 128:(p + 1) * 128, h * 2048:(h + 1) * 2048])
                S.op('vector', lambda e, h=h: e.tensor_copy(qtb[:, h * 2048:(h + 1) * 2048], stage[:, :]))
            for h in range(2):
                for j in range(16):
                    kt = h * 16 + j
                    S.dma(stage[:, j * 128:(j + 1) * 128],
                          vp[p * 4096 + kt * 128: p * 4096 + kt * 128 + 128, :], wait=(j == 0))
                S.op('vector', lambda e, h=h: e.tensor_copy(vsb[:, h * 2048:(h + 1) * 2048], stage[:, :]))

            for qc in range(4):
                for kt in range(32):
                    S.op('tensor', lambda e, kt=kt, qc=qc: e.matmul(
                        PSa[:, :], ktb[:, kt * 128:(kt + 1) * 128], qtb[:, qc * 1024: qc * 1024 + 512],
                        start=True, stop=True))
                    S.op('tensor', lambda e, kt=kt, qc=qc: e.matmul(
                        PSb[:, :], ktb[:, kt * 128:(kt + 1) * 128], qtb[:, qc * 1024 + 512: qc * 1024 + 1024],
                        start=True, stop=True), wait=False)
                    S.op('scalar', lambda e: e.activation(expb[:, 0:512], PSa[:, :], AF.Exp, scale=SCALE))
                    S.op('scalar', lambda e: e.activation(expb[:, 512:1024], PSb[:, :], AF.Exp, scale=SCALE), wait=False)
                    S.op('tensor', lambda e, kt=kt: e.matmul(
                        PD0[0:1, 0:512], onesb[:, 0:1], expb[:, 0:512],
                        start=(kt == 0), stop=(kt == 31), skip_group_check=True))
                    S.op('tensor', lambda e, kt=kt: e.matmul(
                        PD1[0:1, 0:512], onesb[:, 0:1], expb[:, 512:1024],
                        start=(kt == 0), stop=(kt == 31), skip_group_check=True), wait=False)
                    for qt in range(8):
                        PO = POa if qt < 4 else POb
                        co = (qt % 4) * 128
                        S.op('tensor', lambda e, kt=kt, qt=qt, PO=PO, co=co: e.matmul(
                            PO[:, co:co + 128], expb[:, qt * 128:(qt + 1) * 128], vsb[:, kt * 128:(kt + 1) * 128],
                            start=(kt == 0), stop=(kt == 31), skip_group_check=True), wait=False)
                S.op('vector', lambda e: e.tensor_copy(dn[0:1, 0:512], PD0[0:1, 0:512]))
                S.op('vector', lambda e: e.tensor_copy(dn[0:1, 512:1024], PD1[0:1, 0:512]))
                for qt in range(8):
                    PO = POa if qt < 4 else POb
                    co = (qt % 4) * 128
                    S.op('tensor', lambda e, qt=qt: e.transpose(PTq[:, 0:1], dn[0:1, qt * 128:(qt + 1) * 128], idf[0:1, 0:1]))
                    S.op('scalar', lambda e: e.copy(d1[:, 0:1], PTq[:, 0:1]))
                    S.op('vector', lambda e: e.reciprocal(d2[:, 0:1], d1[:, 0:1]))
                    S.op('vector', lambda e, PO=PO, co=co: e.tensor_single_scalar(oev[:, :], PO[:, co:co + 128], d2[:, 0:1], ALU.mult))
                    S.dma(Op[p * 4096 + qc * 1024 + qt * 128: p * 4096 + qc * 1024 + qt * 128 + 128, :], oev[:, :])
        S.fin()
        _emit(nc, S, es)
    return nc


def build_p4c():
    nc = bass.Bass("TRN2", target_bir_lowering=False)
    attnT = nc.dram_tensor("attnT", [D, TOK], F32, kind="ExternalInput").ap()
    xn = nc.dram_tensor("xn", [TOK, D], F32, kind="ExternalInput").ap()
    woT = nc.dram_tensor("woT", [D, D], F32, kind="ExternalInput").ap()
    out = nc.dram_tensor("out", [TOK, D], F32, kind="ExternalOutput").ap()

    es = contextlib.ExitStack()
    with es:
        sb = lambda name, shape, dt: es.enter_context(nc.sbuf_tensor(name, shape, dt))
        ps = lambda name, shape, dt=F32: es.enter_context(nc.psum_tensor(name, shape, dt))
        b = B()
        b.stage = sb("stage", [128, 2048], F32)
        b.wtb = sb("wtb", [128, NKD * 512], BF16)
        b.PGs = [ps("PGa", [128, 512]), ps("PGb", [128, 512])]; b.pg_marks = {}
        attb = sb("attb", [128, NKD * 1024], BF16)
        xq = sb("xq", [128, 512], F32); ev = sb("ev", [128, 512], F32)

        S = Seq()
        emit_consts(nc, es, S, LN_CONSTS)
        load_conv(S, b, attnT, attb, NKD * 1024, 1024)

        def post(tt, oc, PG):
            S.dma(xq[:, :], xn[tt * 128:(tt + 1) * 128, oc * 512:(oc + 1) * 512])
            S.op('vector', lambda e, PG=PG: e.tensor_add(ev[:, :], PG[:, :], xq[:, :]))
            S.dma(out[tt * 128:(tt + 1) * 128, oc * 512:(oc + 1) * 512], ev[:, :])

        emit_gemm(S, woT, 8, b, lambda tt, dt: attb[:, dt * 1024 + tt * 128: dt * 1024 + tt * 128 + 128],
                  post=post)
        S.fin()
        _emit(nc, S, es)
    return nc


def build_p5():
    nc = bass.Bass("TRN2", target_bir_lowering=False)
    xn = nc.dram_tensor("xn", [TOK, D], F32, kind="ExternalInput").ap()
    wT = nc.dram_tensor("wT", [D, D], F32, kind="ExternalInput").ap()
    scb = nc.dram_tensor("scb", [128, 1024], F32, kind="ExternalInput").ap()
    idb_h = nc.dram_tensor("idb", [128, 128], BF16, kind="ExternalInput").ap()
    out = nc.dram_tensor("out", [TOK, D], F32, kind="ExternalOutput").ap()

    es = contextlib.ExitStack()
    with es:
        b = B()
        sb = _alloc_common(nc, es, b)
        ps = lambda name, shape, dt=F32: es.enter_context(nc.psum_tensor(name, shape, dt))
        b.PGs = [ps("PGa", [128, 512]), ps("PGb", [128, 512])]; b.pg_marks = {}
        b.PT = ps("PT", [128, 512], BF16)
        b.sc = sb("sc_sb", [128, 1024], F32)
        xq = sb("xq", [128, 512], F32)
        ev5 = sb("ev5", [128, 512], F32)

        S = Seq()
        emit_consts(nc, es, S, LN_CONSTS)
        S.dma(b.idb[:, :], idb_h[:, :])
        S.dma(b.sc[:, :], scb[:, :], wait=False)
        emit_ln_phase(S, xn, b)

        def post(tt, oc, PG):
            S.dma(xq[:, :], xn[tt * 128:(tt + 1) * 128, oc * 512:(oc + 1) * 512])
            S.op('vector', lambda e, PG=PG: e.scalar_tensor_tensor(
                ev5[:, :], PG[:, :], 0.5, xq[:, :], ALU.mult, ALU.add))
            S.dma(out[tt * 128:(tt + 1) * 128, oc * 512:(oc + 1) * 512], ev5[:, :])

        emit_gemm(S, wT, 8, b, lambda tt, dt: b.lnT[:, dt * 1024 + tt * 128: dt * 1024 + tt * 128 + 128],
                  post=post, dequant=True)
        S.fin()
        _emit(nc, S, es)
    return nc


_CACHE = {}


def _prog(name, builder):
    if name not in _CACHE:
        _CACHE[name] = builder()
    return _CACHE[name]


def _run(nc, maps):
    res = run_bass_kernel_spmd(nc, maps, core_ids=list(range(8)))
    return res.results


def _T(a):
    return np.ascontiguousarray(np.asarray(a, np.float32).T)


def _bc(v):
    v = np.asarray(v, np.float32).reshape(-1)
    return np.ascontiguousarray(np.broadcast_to(v[None, :], (128, v.shape[0])))


def kernel(**inp):
    f32 = lambda k: np.asarray(inp[k], np.float32)
    x = np.ascontiguousarray(f32("L0_attn_hidden_states").reshape(8192, 4096))

    idb = np.eye(128, dtype=BFDT)
    idf = np.eye(128, dtype=np.float32)
    onesf = np.ones((128, 1), np.float32)
    onesr = np.ones((1, 128), np.float32)
    onesb = np.ones((128, 1), BFDT)

    wT0 = _T(f32("L0_ffn_weight"))
    gw1T = _T(f32("L0_ffn_gw1"))
    gw2T = _T(f32("L0_ffn_gw2"))
    gb1b = _bc(f32("L0_ffn_gb1")); gamb = _bc(f32("L0_ffn_ln_gamma"))
    betb = _bc(f32("L0_ffn_ln_beta")); gb2b = _bc(f32("L0_ffn_gb2"))
    lAf = np.ascontiguousarray(np.transpose(f32("L0_ffn_lora_A"), (1, 0, 2)).reshape(4096, 128))
    lBf = np.ascontiguousarray(f32("L0_ffn_lora_B").reshape(128, 4096))
    wqT0 = _T(f32("L0_attn_Wq")); wkT0 = _T(f32("L0_attn_Wk")); wvT0 = _T(f32("L0_attn_Wv"))
    woT0 = _T(f32("L0_attn_Wo"))
    bqb = _bc(f32("L0_attn_bq")); bkb = _bc(f32("L0_attn_bk")); bvb = _bc(f32("L0_attn_bv"))
    bob = _bc(f32("L0_attn_bo"))
    wqT1 = _T(f32("L1_attn_Wq")); wkT1 = _T(f32("L1_attn_Wk")); wvT1 = _T(f32("L1_attn_Wv"))
    woT1 = _T(f32("L1_attn_Wo"))
    wT1 = _T(f32("L1_ffn_weight"))
    sc = f32("L1_ffn_weight_scale")
    sc_host = np.empty((128, 1024), np.float32)
    for dt in range(32):
        for g in range(32):
            sc_host[:, dt * 32 + g] = sc[g, dt]
    inv = 1.0 / (10000.0 ** (np.arange(0, 128, 2, dtype=np.float32) / 128.0))
    t = np.arange(4096, dtype=np.float32)
    freqs = t[:, None] * inv[None, :]
    emb = np.concatenate([freqs, freqs], axis=-1)
    cos_t = np.cos(emb).astype(np.float32)
    sin_t = np.sin(emb).astype(np.float32)

    P1 = _prog("p1", build_p1); P2 = _prog("p2", build_p2)
    P3a = _prog("p3a", build_p3a); P3b = _prog("p3b", build_p3b)
    P4a = _prog("p4a", build_p4a); P4b = _prog("p4b", build_p4b)
    P4c = _prog("p4c", build_p4c); P5 = _prog("p5", build_p5)

    def shards(xf):
        return [np.ascontiguousarray(xf[c * 1024:(c + 1) * 1024]) for c in range(8)]

    def ffn0(xf):
        xs = shards(xf)
        r1 = _run(P1, [dict(xn=xs[c], wT=wT0, gw1T=gw1T, gw2T=gw2T, gb1b=gb1b, gamb=gamb,
                            betb=betb, gb2b=gb2b, lAf=lAf, idb=idb, onesf=onesf) for c in range(8)])
        pcall = np.ascontiguousarray(np.concatenate([r1[c]["pc"] for c in range(8)], axis=0))
        r2 = _run(P2, [dict(xn=xs[c], main=r1[c]["main"], tmp=r1[c]["tmp"], wg=r1[c]["wg"],
                            pcall=pcall, lBf=lBf, idb=idb, onesf=onesf, onesr=onesr) for c in range(8)])
        return np.concatenate([r2[c]["out"] for c in range(8)], axis=0)

    def attn0(xf):
        xs = shards(xf)
        r1 = _run(P3a, [dict(xn=xs[c], wqT=wqT0, wkT=wkT0, wvT=wvT0, bqb=bqb, bkb=bkb, bvb=bvb,
                             idb=idb, onesb=onesb) for c in range(8)])
        maps = []
        for c in range(8):
            bb = c // 4
            Mall = np.ascontiguousarray(np.concatenate([r1[4 * bb + j]["Mp"] for j in range(4)], axis=0))
            ksall = np.ascontiguousarray(np.concatenate([r1[4 * bb + j]["ksp"] for j in range(4)], axis=0))
            maps.append(dict(qT=_T(r1[c]["q"]), xn=xs[c], Mall=Mall, ksall=ksall, woT=woT0,
                             bob=bob, idb=idb, idf=idf, onesf=onesf))
        r2 = _run(P3b, maps)
        return np.concatenate([r2[c]["out"] for c in range(8)], axis=0)

    def ffn1(xf):
        xs = shards(xf)
        r = _run(P5, [dict(xn=xs[c], wT=wT1, scb=sc_host, idb=idb) for c in range(8)])
        return np.concatenate([r[c]["out"] for c in range(8)], axis=0)

    def attn1(xf):
        xs = shards(xf)
        r1 = _run(P4a, [dict(xn=xs[c], wqT=wqT1, wkT=wkT1, wvT=wvT1,
                             cosc=np.ascontiguousarray(cos_t[(c % 4) * 1024:(c % 4 + 1) * 1024]),
                             sinc=np.ascontiguousarray(sin_t[(c % 4) * 1024:(c % 4 + 1) * 1024]),
                             idb=idb) for c in range(8)])
        Q = np.concatenate([r1[c]["q"] for c in range(8)], axis=0).reshape(2, 4096, 4096)
        K = np.concatenate([r1[c]["k"] for c in range(8)], axis=0).reshape(2, 4096, 4096)
        V = np.concatenate([r1[c]["v"] for c in range(8)], axis=0).reshape(2, 4096, 4096)
        maps = []
        for c in range(8):
            qT_l, kT_l, v_l = [], [], []
            for j in range(8):
                f = c * 8 + j
                bb, h = f // 32, f % 32
                qT_l.append(_T(Q[bb, :, h * 128:(h + 1) * 128]))
                kT_l.append(_T(K[bb, :, h * 128:(h + 1) * 128]))
                v_l.append(np.ascontiguousarray(V[bb, :, h * 128:(h + 1) * 128]))
            maps.append(dict(qTp=np.ascontiguousarray(np.concatenate(qT_l, 0)),
                             kTp=np.ascontiguousarray(np.concatenate(kT_l, 0)),
                             vp=np.ascontiguousarray(np.concatenate(v_l, 0)),
                             idf=idf, onesb=onesb))
        r2 = _run(P4b, maps)
        A = np.empty((2, 4096, 4096), np.float32)
        for c in range(8):
            for j in range(8):
                f = c * 8 + j
                bb, h = f // 32, f % 32
                A[bb, :, h * 128:(h + 1) * 128] = r2[c]["Op"][j * 4096:(j + 1) * 4096, :]
        Af = A.reshape(8192, 4096)
        maps = [dict(attnT=_T(Af[c * 1024:(c + 1) * 1024]), xn=xs[c], woT=woT1) for c in range(8)]
        r3 = _run(P4c, maps)
        return np.concatenate([r3[c]["out"] for c in range(8)], axis=0)

    x = ffn0(x)
    x = attn0(x)
    x = ffn0(x)
    x = ffn1(x)
    x = attn1(x)
    x = ffn1(x)
    return x.reshape(2, 4096, 4096)


if __name__ == "__main__":
    import reference as R
    inputs = R.setup_inputs(0)
    out = kernel(**{k: np.asarray(v) for k, v in inputs.items()})
    ref = np.asarray(R.reference(**inputs))
    err = np.sqrt(np.mean((out - ref) ** 2) / np.mean(ref ** 2))
    print("rel l2:", err)
